# version 05 (final)
# speedup vs baseline: 1.5885x; 1.1721x over previous
import math
from contextlib import ExitStack
import numpy as np
import ml_dtypes
import concourse.bass as bass
import concourse.mybir as mybir
from concourse.bass_utils import run_bass_kernel_spmd

F32 = mybir.dt.float32
BF16 = mybir.dt.bfloat16
I32 = mybir.dt.int32
AF = mybir.ActivationFunctionType
ALU = mybir.AluOpType
AX = mybir.AxisListType
IOX = bass.IndirectOffsetOnAxis

NB = np.dtype(ml_dtypes.bfloat16)

B, S, D = 4, 8192, 1024
T = B * S
TPC = T // 8
ECAP = T * 2 // 8

_nc_cache = {}


def _cached(name, fn):
    if name not in _nc_cache:
        _nc_cache[name] = fn()
    return _nc_cache[name]


class Eng:

    def __init__(self, raw, sem):
        self.raw = raw
        self.sem = sem
        self.n = 0

    def do(self, inst):
        self.n += 1
        inst.then_inc(self.sem, 1)
        self.raw.wait_ge(self.sem, self.n)
        return self.n

    def tag(self, inst):
        self.n += 1
        inst.then_inc(self.sem, 1)
        return self.n

    def wait(self, other_eng, count):
        if count > 0:
            self.raw.wait_ge(other_eng.sem, count)

    def __getattr__(self, name):
        return getattr(self.raw, name)


class Dma:

    def __init__(self, raw, sem):
        self.raw = raw
        self.sem = sem
        self.n = 0

    def start(self, out, in_, **kw):
        self.raw.dma_start(out=out, in_=in_, **kw).then_inc(self.sem, 16)
        self.n += 16
        return self.n

    def indirect(self, out, out_offset, in_, in_offset, **kw):
        self.raw.indirect_dma_start(out=out, out_offset=out_offset, in_=in_,
                                    in_offset=in_offset, **kw).then_inc(self.sem, 16)
        self.n += 16
        return self.n

    def wait(self, eng, count=None):
        eng.raw.wait_ge(self.sem, self.n if count is None else count)

    def load(self, out, in_, eng=None, **kw):
        c = self.start(out, in_, **kw)
        (eng or self).raw.wait_ge(self.sem, c)
        return c


def make_identity(g, ident):
    g.memset(ident, 0.0)
    g.affine_select(out=ident, in_=ident, compare_op=ALU.not_equal, fill=1.0,
                    base=0, pattern=[[-1, ident.shape[0]]], channel_multiplier=1)


def build_ln(rows):
    nc = bass.Bass("TRN2", target_bir_lowering=False)
    x = nc.dram_tensor("x", [rows, D], F32, kind="ExternalInput").ap()
    xn = nc.dram_tensor("xn", [rows, D], F32, kind="ExternalOutput").ap()
    nt = rows // 128
    with (
        nc.sbuf_tensor("xt", [128, 2 * D], F32) as xt,
        nc.sbuf_tensor("stats", [128, 12], F32) as stats,
        nc.sbuf_tensor("mv", [128, 2], F32) as mv,
        nc.sbuf_tensor("rstd", [128, 1], F32) as rstd,
        nc.sbuf_tensor("ot", [128, 2 * D], F32) as ot,
        nc.semaphore("ld") as ld_s,
        nc.semaphore("vs") as vs,
        nc.semaphore("ss") as ss,
        nc.semaphore("st") as st_s,
        nc.Block() as block,
    ):
        vn = [0]
        sn = [0]

        @block.sync
        def _(sync):
            for i in range(nt):
                b = (i % 2) * D
                if i >= 2:
                    sync.wait_ge(vs, 6 * (i - 2) + 6)
                sync.dma_start(out=xt[:, b:b + D],
                               in_=x[i * 128:(i + 1) * 128, :]).then_inc(ld_s, 16)

        @block.vector
        def _(v):
            V = Eng(v, vs)
            for i in range(nt):
                b = (i % 2) * D
                v.wait_ge(ld_s, 16 * (i + 1))
                V.do(v.bn_stats(stats[:, 0:6], xt[:, b:b + 512]))
                V.do(v.bn_stats(stats[:, 6:12], xt[:, b + 512:b + D]))
                V.do(v.bn_aggr(mv[:, 0:2], stats[:, 0:12]))
                V.do(v.tensor_scalar(mv[:, 1:2], mv[:, 1:2], 1e-5, None, ALU.add))
                v.wait_ge(ss, i + 1)
                V.do(v.reciprocal(rstd[:, 0:1], rstd[:, 0:1]))
                if i >= 2:
                    v.wait_ge(st_s, 16 * (i - 1))
                V.do(v.tensor_scalar(ot[:, b:b + D], xt[:, b:b + D],
                                     mv[:, 0:1], rstd[:, 0:1],
                                     ALU.subtract, ALU.mult))
            vn[0] = V.n

        @block.scalar
        def _(sc):
            for i in range(nt):
                sc.wait_ge(vs, 6 * i + 4)
                sc.activation(out=rstd[:, 0:1], in_=mv[:, 1:2],
                              func=AF.Sqrt).then_inc(ss, 1)

        @block.gpsimd
        def _(g):
            for i in range(nt):
                b = (i % 2) * D
                g.wait_ge(vs, 6 * i + 6)
                g.dma_start(out=xn[i * 128:(i + 1) * 128, :],
                            in_=ot[:, b:b + D]).then_inc(st_s, 16)
            g.wait_ge(st_s, 16 * nt)
    return nc


def build_router(cols):
    nc = bass.Bass("TRN2", target_bir_lowering=False)
    xnT = nc.dram_tensor("xnT", [D, cols], F32, kind="ExternalInput").ap()
    wr = nc.dram_tensor("wr", [D, 8], F32, kind="ExternalInput").ap()
    lg = nc.dram_tensor("lg", [8, cols], F32, kind="ExternalOutput").ap()
    ntc = cols // 512
    with (
        nc.sbuf_tensor("wt", [128, 8 * 8], F32) as wt,
        nc.sbuf_tensor("xc", [128, 2 * 8 * 512], F32) as xc,
        nc.sbuf_tensor("lt", [8, 512], F32) as lt,
        nc.psum_tensor("lp", [8, 512], F32) as lp,
        nc.semaphore("ld") as ld_s,
        nc.semaphore("pe") as pe_s,
        nc.semaphore("vs") as vs,
        nc.semaphore("st") as st_s,
        nc.Block() as block,
    ):
        @block.sync
        def _(sync):
            sync.dma_start(out=wt[:].rearrange("p (c e) -> p c e", e=8),
                           in_=wr.rearrange("(c p) e -> p c e", p=128)).then_inc(ld_s, 16)
            for i in range(ntc):
                b = (i % 2) * 8 * 512
                if i >= 2:
                    sync.wait_ge(pe_s, i - 1)
                sync.dma_start(
                    out=xc[:, b:b + 8 * 512].rearrange("p (c t) -> p c t", t=512),
                    in_=xnT[:, i * 512:(i + 1) * 512].rearrange("(c p) t -> p c t", p=128),
                ).then_inc(ld_s, 16)

        @block.tensor
        def _(tn):
            for i in range(ntc):
                b = (i % 2) * 8 * 512
                tn.wait_ge(ld_s, 16 * (i + 2))
                if i > 0:
                    tn.wait_ge(vs, i)
                for c in range(8):
                    mm = tn.matmul(lp[:], wt[:, c * 8:(c + 1) * 8],
                                   xc[:, b + c * 512:b + (c + 1) * 512],
                                   start=(c == 0), stop=(c == 7))
                mm.then_inc(pe_s, 1)

        @block.vector
        def _(v):
            for i in range(ntc):
                v.wait_ge(pe_s, i + 1)
                v.wait_ge(st_s, 16 * i)
                v.tensor_copy(lt[:], lp[:]).then_inc(vs, 1)

        @block.gpsimd
        def _(g):
            for i in range(ntc):
                g.wait_ge(vs, i + 1)
                g.dma_start(out=lg[:, i * 512:(i + 1) * 512], in_=lt[:]).then_inc(st_s, 16)
            g.wait_ge(st_s, 16 * ntc)
    return nc


def build_routing():
    NT = T // 128
    GB = 16
    NB_ = NT // GB
    SRCN = 65664
    nc = bass.Bass("TRN2", target_bir_lowering=False)
    lg = nc.dram_tensor("lg", [128, NT * 8], F32, kind="ExternalInput").ap()
    slot1_o = nc.dram_tensor("slot1", [128, NT], I32, kind="ExternalOutput").ap()
    slot2_o = nc.dram_tensor("slot2", [128, NT], I32, kind="ExternalOutput").ap()
    w1_o = nc.dram_tensor("w1h", [128, NT], F32, kind="ExternalOutput").ap()
    w2_o = nc.dram_tensor("w2h", [128, NT], F32, kind="ExternalOutput").ap()
    srcmap_o = nc.dram_tensor("srcmap", [SRCN, 1], I32, kind="ExternalOutput").ap()
    with ExitStack() as ctx:
        sb = lambda n, s, d=F32: ctx.enter_context(nc.sbuf_tensor(n, s, d))
        psum = lambda n, s, d=F32: ctx.enter_context(nc.psum_tensor(n, s, d))
        sem = lambda n: ctx.enter_context(nc.semaphore(n))
        lt_all = sb("lt_all", [128, NT * 8])
        iota8 = sb("iota8", [128, 8])
        ident = sb("ident", [128, 128])
        m1t = sb("m1t", [128, GB]); m2t = sb("m2t", [128, GB])
        z16 = sb("z16", [128, GB]); e2t = sb("e2t", [128, GB])
        dt16 = sb("dt16", [128, GB * 8]); et16 = sb("et16", [128, GB * 8])
        oh1b = sb("oh1b", [128, GB * 8]); oh2b = sb("oh2b", [128, GB * 8])
        lt2b = sb("lt2b", [128, GB * 8]); cntb = sb("cntb", [128, GB * 8])
        tscb = sb("tscb", [128, GB * 8])
        cntT = sb("cntT", [8, T])
        ones8 = sb("ones8", [8, 4096])
        scanb = sb("scanb", [8, 4096]); carry = sb("carry", [8, 1])
        e1a = sb("e1a", [128, NT]); e2a = sb("e2a", [128, NT])
        w1a = sb("w1a", [128, NT]); w2a = sb("w2a", [128, NT])
        ctok = sb("ctok", [128, GB * 8])
        o1t = sb("o1t", [128, GB]); o2t = sb("o2t", [128, GB])
        v1t = sb("v1t", [128, GB]); v2t = sb("v2t", [128, GB])
        tr1 = sb("tr1", [128, GB])
        sl1 = sb("sl1", [128, NT]); sl2 = sb("sl2", [128, NT])
        sl1i = sb("sl1i", [128, NT], I32); sl2i = sb("sl2i", [128, NT], I32)
        tokid = sb("tokid", [128, NT], I32)
        tp = psum("tp", [8, 128]); tq = psum("tq", [128, 8])
        ld_s = sem("ld"); vs = sem("vs"); ss = sem("ss")
        ps = sem("ps"); gs = sem("gs"); st_s = sem("st")
        block = ctx.enter_context(nc.Block())
        VA = 17 + 16
        VC = 38
        NSCAN = 1 + 3 * (T // 4096)

        def t3(ap_):
            return ap_.rearrange("p (t e) -> p t e", e=8)

        def bc(col_ap):
            return col_ap.rearrange("p (t one) -> p t one", one=1).broadcast_to([128, GB, 8])

        @block.sync
        def _(sync):
            sync.dma_start(out=lt_all[:], in_=lg[:]).then_inc(ld_s, 16)

        @block.vector
        def _(v):
            V = Eng(v, vs)
            v.wait_ge(gs, 6)
            v.wait_ge(ld_s, 16)
            iob = iota8[:].rearrange("p (one e) -> p one e", one=1).broadcast_to([128, GB, 8])
            for b in range(NB_):
                lt = lt_all[:, b * GB * 8:(b + 1) * GB * 8]
                V.do(v.reduce_max(m1t[:].rearrange("p (t o) -> p t o", o=1), t3(lt), AX.X))
                V.do(v.tensor_tensor(t3(dt16[:]), t3(lt), bc(m1t[:]), ALU.subtract))
                v.wait_ge(ss, 2 * b + 1)
                V.do(v.reduce_sum(z16[:].rearrange("p (t o) -> p t o", o=1), t3(et16[:]), AX.X))
                V.do(v.tensor_tensor(t3(oh1b[:]), t3(lt), bc(m1t[:]), ALU.is_ge))
                V.do(v.scalar_tensor_tensor(lt2b[:], oh1b[:], -1e30, lt, ALU.mult, ALU.add))
                V.do(v.reduce_max(m2t[:].rearrange("p (t o) -> p t o", o=1), t3(lt2b[:]), AX.X))
                V.do(v.tensor_tensor(t3(oh2b[:]), t3(lt2b[:]), bc(m2t[:]), ALU.is_ge))
                V.do(v.reciprocal(z16[:], z16[:]))
                V.do(v.tensor_scalar(w1a[:, b * GB:(b + 1) * GB], z16[:], 0.5, None, ALU.mult))
                V.do(v.tensor_sub(m2t[:], m2t[:], m1t[:]))
                v.wait_ge(ss, 2 * b + 2)
                V.do(v.tensor_mul(e2t[:], e2t[:], z16[:]))
                V.do(v.tensor_scalar(w2a[:, b * GB:(b + 1) * GB], e2t[:], 0.5, None, ALU.mult))
                V.do(v.tensor_tensor(t3(tscb[:]), t3(oh1b[:]), iob, ALU.mult))
                V.do(v.reduce_sum(e1a[:, b * GB:(b + 1) * GB].rearrange("p (t o) -> p t o", o=1),
                                  t3(tscb[:]), AX.X))
                V.do(v.tensor_tensor(t3(tscb[:]), t3(oh2b[:]), iob, ALU.mult))
                V.do(v.reduce_sum(e2a[:, b * GB:(b + 1) * GB].rearrange("p (t o) -> p t o", o=1),
                                  t3(tscb[:]), AX.X))
                V.do(v.tensor_add(cntb[:], oh1b[:], oh2b[:]))
                for k in range(GB):
                    i = b * GB + k
                    v.wait_ge(ps, i + 1)
                    V.do(v.tensor_copy(cntT[:, i * 128:(i + 1) * 128], tp[:]))
            assert V.n == VA * NB_, V.n
            V.do(v.memset(carry[:], 0.0))
            for c in range(T // 4096):
                sl = cntT[:, c * 4096:(c + 1) * 4096]
                V.do(v.tensor_tensor_scan(scanb[:], ones8[:], sl, carry[:, 0:1],
                                          ALU.mult, ALU.add))
                V.do(v.tensor_copy(carry[:], scanb[:, 4095:4096]))
                V.do(v.tensor_sub(sl, scanb[:], sl))
            base = VA * NB_ + NSCAN
            assert V.n == base, V.n
            for b in range(NB_):
                cols = slice(b * GB, (b + 1) * GB)
                for k in range(GB):
                    i = b * GB + k
                    v.wait_ge(ps, NT + i + 1)
                    V.do(v.tensor_copy(ctok[:, k * 8:(k + 1) * 8], tq[:]))
                V.do(v.tensor_tensor(t3(tscb[:]), iob, bc(e1a[:, cols]), ALU.is_equal))
                V.do(v.tensor_mul(tscb[:], tscb[:], ctok[:]))
                V.do(v.reduce_sum(o1t[:].rearrange("p (t o) -> p t o", o=1), t3(tscb[:]), AX.X))
                V.do(v.tensor_tensor(t3(tscb[:]), iob, bc(e2a[:, cols]), ALU.is_equal))
                V.do(v.tensor_mul(tscb[:], tscb[:], ctok[:]))
                V.do(v.reduce_sum(o2t[:].rearrange("p (t o) -> p t o", o=1), t3(tscb[:]), AX.X))
                V.do(v.tensor_scalar(sl1[:, cols], e1a[:, cols], 8192.0, None, ALU.mult))
                V.do(v.tensor_add(sl1[:, cols], sl1[:, cols], o1t[:]))
                V.do(v.tensor_scalar(sl2[:, cols], e2a[:, cols], 8192.0, None, ALU.mult))
                V.do(v.tensor_add(sl2[:, cols], sl2[:, cols], o2t[:]))
                V.do(v.tensor_scalar(v1t[:], o1t[:], float(ECAP), None, ALU.is_lt))
                V.do(v.tensor_scalar(v2t[:], o2t[:], float(ECAP), None, ALU.is_lt))
                V.do(v.tensor_mul(sl1[:, cols], sl1[:, cols], v1t[:]))
                V.do(v.tensor_mul(sl2[:, cols], sl2[:, cols], v2t[:]))
                V.do(v.tensor_scalar(tr1[:], v1t[:], 1.0, float(SRCN - 1), ALU.subtract, ALU.mult))
                V.do(v.tensor_sub(sl1[:, cols], sl1[:, cols], tr1[:]))
                V.do(v.tensor_scalar(tr1[:], v2t[:], 1.0, float(SRCN - 1), ALU.subtract, ALU.mult))
                V.do(v.tensor_sub(sl2[:, cols], sl2[:, cols], tr1[:]))
                V.do(v.tensor_mul(w1a[:, cols], w1a[:, cols], v1t[:]))
                V.do(v.tensor_mul(w2a[:, cols], w2a[:, cols], v2t[:]))
                V.do(v.tensor_copy(sl1i[:, cols], sl1[:, cols]))
                V.do(v.tensor_copy(sl2i[:, cols], sl2[:, cols]))
            assert V.n == base + VC * NB_, V.n

        @block.scalar
        def _(sc):
            for b in range(NB_):
                sc.wait_ge(vs, VA * b + 2)
                sc.activation(out=et16[:], in_=dt16[:], func=AF.Exp).then_inc(ss, 1)
                sc.wait_ge(vs, VA * b + 10)
                sc.activation(out=e2t[:], in_=m2t[:], func=AF.Exp).then_inc(ss, 1)

        @block.tensor
        def _(tn):
            tn.wait_ge(gs, 4)
            for b in range(NB_):
                tn.wait_ge(vs, VA * b + 17)
                for k in range(GB):
                    i = b * GB + k
                    if i > 0:
                        tn.wait_ge(vs, VA * b + 17 + k)
                    tn.transpose(tp[:], cntb[:, k * 8:(k + 1) * 8], ident[:]).then_inc(ps, 1)
            base = VA * NB_ + NSCAN
            for b in range(NB_):
                tn.wait_ge(vs, base)
                for k in range(GB):
                    i = b * GB + k
                    if i > 0:
                        tn.wait_ge(vs, base + VC * b + k)
                    tn.transpose(tq[:], cntT[:, i * 128:(i + 1) * 128],
                                 ident[0:8, 0:8]).then_inc(ps, 1)

        @block.gpsimd
        def _(g):
            G = Eng(g, gs)
            G.do(g.memset(iota8[:], 0.0))
            G.do(g.iota(iota8[:], pattern=[[1, 8]], base=0, channel_multiplier=0,
                        allow_small_or_imprecise_dtypes=True))
            G.do(g.memset(ident[:], 0.0))
            G.do(g.affine_select(out=ident[:], in_=ident[:], compare_op=ALU.not_equal,
                                 fill=1.0, base=0, pattern=[[-1, 128]], channel_multiplier=1))
            G.do(g.memset(ones8[:], 1.0))
            G.do(g.iota(tokid[:], pattern=[[128, NT]], base=0, channel_multiplier=1,
                        allow_small_or_imprecise_dtypes=True))
            vend = (17 + 16) * NB_ + NSCAN + 38 * NB_
            g.wait_ge(vs, vend)
            for i in range(NT):
                g.indirect_dma_start(
                    out=srcmap_o, out_offset=IOX(ap=sl1i[:, i:i + 1], axis=0),
                    in_=tokid[:, i:i + 1], in_offset=None,
                ).then_inc(st_s, 16)
                g.indirect_dma_start(
                    out=srcmap_o, out_offset=IOX(ap=sl2i[:, i:i + 1], axis=0),
                    in_=tokid[:, i:i + 1], in_offset=None,
                ).then_inc(st_s, 16)
            g.dma_start(out=slot1_o[:], in_=sl1i[:]).then_inc(st_s, 16)
            g.dma_start(out=slot2_o[:], in_=sl2i[:]).then_inc(st_s, 16)
            g.dma_start(out=w1_o[:], in_=w1a[:]).then_inc(st_s, 16)
            g.dma_start(out=w2_o[:], in_=w2a[:]).then_inc(st_s, 16)
            g.wait_ge(st_s, 16 * (2 * NT + 4))
    return nc


def build_moe_expert():
    FF = 4096
    NG = ECAP // 512
    nc = bass.Bass("TRN2", target_bir_lowering=False)
    xn = nc.dram_tensor("xn", [T, D], F32, kind="ExternalInput").ap()
    src = nc.dram_tensor("src", [128, ECAP // 128], I32, kind="ExternalInput").ap()
    w1 = nc.dram_tensor("w1", [D, FF], F32, kind="ExternalInput").ap()
    w2 = nc.dram_tensor("w2", [FF, D], F32, kind="ExternalInput").ap()
    yo = nc.dram_tensor("y", [ECAP, D], BF16, kind="ExternalOutput").ap()
    C1 = 0.7978845608028654
    C3 = 0.044715 * C1
    with ExitStack() as ctx:
        sb = lambda n, s, d=F32: ctx.enter_context(nc.sbuf_tensor(n, s, d))
        psum = lambda n, s, d=F32: ctx.enter_context(nc.psum_tensor(n, s, d))
        sem = lambda n: ctx.enter_context(nc.semaphore(n))
        w1b = sb("w1b", [128, 8 * FF], BF16)
        w2b = sb("w2b", [128, 32 * D], BF16)
        srct = sb("srct", [128, NG * 4], I32)
        gx = sb("gx", [128, 4 * D], BF16)
        xbT = sb("xbT", [128, 8 * 512], BF16)
        hfull = sb("hfull", [128, 32 * 512], BF16)
        sq = sb("sq", [128, 512])
        t1 = sb("t1", [128, 512])
        uu = sb("uu", [128, 512])
        th = sb("th", [128, 512])
        yb = sb("yb", [128, 2 * D], BF16)
        identb = sb("identb", [128, 128], BF16)
        hp = psum("hp", [128, 2 * 512])
        yp = psum("yp", [128, 2 * 512])
        tpps = [psum(f"tpp{j}", [128, 128], BF16) for j in range(4)]
        ld_s = sem("ld"); gd_s = sem("gd"); vs = sem("vs"); ss = sem("ss")
        pe_s = sem("pe"); st_s = sem("st"); gs = sem("gs")
        block = ctx.enter_context(nc.Block())

        V_TR = 32
        V_F = 3 * 32
        V_Y = 8
        VG = V_TR + V_F + V_Y
        S_F = 2 * 32
        P_TR = 32
        P_H = 32
        P_Y = 8
        PG = P_TR + P_H + P_Y

        @block.sync
        def _(sync):
            sync.dma_start(out=srct[:], in_=src[:]).then_inc(ld_s, 16)

        @block.gpsimd
        def _(g):
            G = Eng(g, gs)
            g.dma_start(out=w1b[:].rearrange("p (c f) -> p c f", f=FF),
                        in_=w1.rearrange("(c p) f -> p c f", p=128)).then_inc(ld_s, 16)
            g.dma_start(out=w2b[:].rearrange("p (c o) -> p c o", o=D),
                        in_=w2.rearrange("(c p) o -> p c o", p=128)).then_inc(ld_s, 16)
            G.do(g.memset(identb[:], 0.0))
            G.do(g.affine_select(out=identb[:], in_=identb[:], compare_op=ALU.not_equal,
                                 fill=1.0, base=0, pattern=[[-1, 128]], channel_multiplier=1))
            g.wait_ge(ld_s, 48)
            for gi in range(NG):
                if gi >= 1:
                    g.wait_ge(pe_s, PG * (gi - 1) + P_TR)
                for i in range(4):
                    g.indirect_dma_start(
                        out=gx[:, i * D:(i + 1) * D],
                        out_offset=None,
                        in_=xn, in_offset=IOX(ap=srct[:, gi * 4 + i:gi * 4 + i + 1], axis=0),
                    ).then_inc(gd_s, 16)

        @block.tensor
        def _(tn):
            tn.wait_ge(gs, 2)
            pc = 0
            for gi in range(NG):
                tn.wait_ge(gd_s, 64 * (gi + 1))
                if gi >= 1:
                    tn.wait_ge(vs, VG * (gi - 1) + V_TR)
                for k in range(32):
                    i, c = divmod(k, 8)
                    if k >= 4:
                        tn.wait_ge(vs, VG * gi + k - 3)
                    tn.transpose(tpps[k % 4][:],
                                 gx[:, i * D + c * 128:i * D + (c + 1) * 128],
                                 identb[:]).then_inc(pe_s, 1)
                    pc += 1
                tn.wait_ge(vs, VG * gi + V_TR)
                for fc in range(32):
                    hb = (fc % 2) * 512
                    if gi * 32 + fc >= 2:
                        pfc = fc - 2
                        pgi = gi
                        if pfc < 0:
                            pfc += 32
                            pgi -= 1
                        tn.wait_ge(vs, VG * pgi + V_TR + 3 * pfc + 3)
                    for c in range(8):
                        mm = tn.matmul(hp[:, hb:hb + 512],
                                       w1b[:, c * FF + fc * 128:c * FF + (fc + 1) * 128],
                                       xbT[:, c * 512:(c + 1) * 512],
                                       start=(c == 0), stop=(c == 7))
                    mm.then_inc(pe_s, 1)
                    pc += 1
                tn.wait_ge(vs, VG * gi + V_TR + V_F)
                for sub in range(4):
                    for half in range(2):
                        if gi + sub + half > 0:
                            tn.wait_ge(vs, max(0, (VG * gi + V_TR + V_F + sub * 2 + half) - 1))
                        for fc in range(32):
                            mm = tn.matmul(yp[:, half * 512:(half + 1) * 512],
                                           hfull[:, fc * 512 + sub * 128:fc * 512 + sub * 128 + 128],
                                           w2b[:, fc * D + half * 512:fc * D + (half + 1) * 512],
                                           start=(fc == 0), stop=(fc == 31))
                        mm.then_inc(pe_s, 1)
                        pc += 1
            assert pc == PG * NG

        @block.vector
        def _(v):
            V = Eng(v, vs)
            for gi in range(NG):
                for i in range(4):
                    for c in range(8):
                        k = i * 8 + c
                        v.wait_ge(pe_s, PG * gi + k + 1)
                        V.do(v.tensor_copy(xbT[:, c * 512 + i * 128:c * 512 + (i + 1) * 128],
                                           tpps[k % 4][:]))
                for fc in range(32):
                    hb = (fc % 2) * 512
                    v.wait_ge(ss, S_F * gi + 2 * fc + 1)
                    V.do(v.tensor_scalar(t1[:], sq[:], C3, C1, ALU.mult, ALU.add))
                    V.do(v.tensor_tensor(uu[:], t1[:], hp[:, hb:hb + 512], ALU.mult))
                    v.wait_ge(ss, S_F * gi + 2 * fc + 2)
                    V.do(v.scalar_tensor_tensor(hfull[:, fc * 512:(fc + 1) * 512],
                                                th[:], 1.0, hp[:, hb:hb + 512], ALU.add, ALU.mult))
                for sub in range(4):
                    for half in range(2):
                        v.wait_ge(pe_s, PG * gi + P_TR + P_H + sub * 2 + half + 1)
                        n = gi * 4 + sub
                        if n >= 2:
                            v.wait_ge(st_s, 16 * (n - 1))
                        V.do(v.tensor_copy(yb[:, (sub % 2) * D + half * 512:(sub % 2) * D + (half + 1) * 512],
                                           yp[:, half * 512:(half + 1) * 512]))

        @block.scalar
        def _(sc):
            SC = Eng(sc, ss)
            for gi in range(NG):
                for fc in range(32):
                    hb = (fc % 2) * 512
                    sc.wait_ge(pe_s, PG * gi + P_TR + fc + 1)
                    SC.do(sc.activation(out=sq[:], in_=hp[:, hb:hb + 512], func=AF.Square))
                    sc.wait_ge(vs, VG * gi + V_TR + 3 * fc + 2)
                    SC.do(sc.activation(out=th[:], in_=uu[:], func=AF.Tanh))
                for sub in range(4):
                    sc.wait_ge(vs, VG * gi + V_TR + V_F + (sub + 1) * 2)
                    sc.dma_start(out=yo[gi * 512 + sub * 128:gi * 512 + (sub + 1) * 128, :],
                                 in_=yb[:, (sub % 2) * D:(sub % 2 + 1) * D]).then_inc(st_s, 16)
            sc.wait_ge(st_s, 16 * 4 * NG)
    return nc


def build_attn():
    SQ = 4096
    NKB = S // 128
    NQC = SQ // 512
    NG = 32
    SCL = 1.0 / math.sqrt(128.0)
    nc = bass.Bass("TRN2", target_bir_lowering=False)
    xnT = nc.dram_tensor("xnT", [D, S], F32, kind="ExternalInput").ap()
    xnTq = nc.dram_tensor("xnTq", [D, SQ], F32, kind="ExternalInput").ap()
    UU = {}
    VV = {}
    bT = {}
    for nm in ("q", "k", "v"):
        UU[nm] = nc.dram_tensor(f"U{nm}", [8, 128, 32], F32, kind="ExternalInput").ap()
        VV[nm] = nc.dram_tensor(f"V{nm}", [8, 32, 128], F32, kind="ExternalInput").ap()
        bT[nm] = nc.dram_tensor(f"b{nm}T", [128, 8], F32, kind="ExternalInput").ap()
    bvrows = nc.dram_tensor("bvrows", [8, 128], F32, kind="ExternalInput").ap()
    so = nc.dram_tensor("s", [D, SQ], BF16, kind="ExternalOutput").ap()
    with ExitStack() as ctx:
        sb = lambda n, s, d=F32: ctx.enter_context(nc.sbuf_tensor(n, s, d))
        psum = lambda n, s, d=F32: ctx.enter_context(nc.psum_tensor(n, s, d))
        sem = lambda n: ctx.enter_context(nc.semaphore(n))
        xk = sb("xk", [128, S], BF16)
        xq = sb("xq", [128, SQ], BF16)
        Vsb = {nm: sb(f"V{nm}sb", [32, 128], BF16) for nm in "qkv"}
        Usb = {nm: sb(f"U{nm}sb", [128, 32], BF16) for nm in "qkv"}
        VTsb = {nm: sb(f"V{nm}T", [128, 32], BF16) for nm in "qkv"}
        UTsb = {nm: sb(f"U{nm}T", [32, 128], BF16) for nm in "qkv"}
        bsb = {nm: sb(f"b{nm}sb", [128, 8]) for nm in "qk"}
        bvrow = sb("bvrow", [1, 128])
        bvb = sb("bvb", [128, 128])
        tq = sb("tq", [32, SQ], BF16)
        tk = sb("tk", [32, S], BF16)
        tv = sb("tv", [32, S], BF16)
        qfm = sb("qfm", [128, SQ], BF16)
        kfm = sb("kfm", [128, S], BF16)
        vsb = sb("vsb", [128, S], BF16)
        Pt = [sb(f"P{j}", [128, 1024], BF16) for j in range(2)]
        onecol = sb("onecol", [128, 1], BF16)
        onerow = sb("onerow", [1, 128])
        rl = sb("rl", [1, 512])
        rlb = sb("rlb", [128, 512])
        on = sb("on", [128, 512], BF16)
        identb = sb("identb", [128, 128], BF16)
        pSd = psum("pSd", [128, 2 * 1024])
        pO = psum("pO", [128, 512])
        pL = psum("pL", [128, 512])
        pT = psum("pT", [128, 128], BF16)
        ld_s = sem("ld"); vs = sem("vs"); ss = sem("ss")
        pe_s = sem("pe"); st_s = sem("st"); gs = sem("gs")
        block = ctx.enter_context(nc.Block())

        P1_PE = 7 + 8 + 16 + 16 + 8 + 16 + 64
        P1_V = 6 + 1 + 8 + 16 + 16 + 8 + 16 + 64
        QC_PE = 3 * NG + 1
        QC_V = 4
        QC_S = NG
        HD_PE = P1_PE + NQC * QC_PE
        HD_V = P1_V + NQC * QC_V
        HD_S = NQC * QC_S

        @block.gpsimd
        def _(g):
            G = Eng(g, gs)
            G.do(g.memset(identb[:], 0.0))
            G.do(g.affine_select(out=identb[:], in_=identb[:], compare_op=ALU.not_equal,
                                 fill=1.0, base=0, pattern=[[-1, 128]], channel_multiplier=1))
            G.do(g.memset(onecol[:], 1.0))
            G.do(g.memset(onerow[:], 1.0))
            for hd in range(8):
                if hd >= 1:
                    g.wait_ge(pe_s, hd * HD_PE)
                g.dma_start(out=xk[:], in_=xnT[hd * 128:(hd + 1) * 128, :]).then_inc(ld_s, 16)
                g.dma_start(out=xq[:], in_=xnTq[hd * 128:(hd + 1) * 128, :]).then_inc(ld_s, 16)
                for nm in "qkv":
                    g.dma_start(out=Vsb[nm][:], in_=VV[nm][hd]).then_inc(ld_s, 16)
                    g.dma_start(out=Usb[nm][:], in_=UU[nm][hd]).then_inc(ld_s, 16)
                for nm in "qk":
                    g.dma_start(out=bsb[nm][:], in_=bT[nm][:]).then_inc(ld_s, 16)
                g.dma_start(out=bvrow[:], in_=bvrows[hd:hd + 1, :]).then_inc(ld_s, 16)

        def p1_slice(idx, kind):
            par = (idx % 2) * 1024
            if kind == "s1":
                return pSd[0:32, par:par + 512]
            if kind == "proj":
                return pSd[:, par + 512:par + 1024]
            return pSd[:, par:par + 128]

        @block.tensor
        def _(tn):
            tn.wait_ge(gs, 4)
            for hd in range(8):
                pe0 = hd * HD_PE
                v0 = hd * HD_V
                tn.wait_ge(ld_s, 16 * 11 * (hd + 1))
                if hd >= 1:
                    tn.wait_ge(vs, hd * HD_V)
                n = pe0
                for nm in "qkv":
                    tn.transpose(pT[:, 0:32], Vsb[nm][:], identb[0:32, 0:32]).then_inc(pe_s, 1)
                    n += 1
                    tn.wait_ge(vs, v0 + (n - pe0))
                    tn.transpose(pT[0:32, 0:128], Usb[nm][:], identb[:]).then_inc(pe_s, 1)
                    n += 1
                    tn.wait_ge(vs, v0 + (n - pe0))
                tn.matmul(pL[:, 0:128], onerow[:], bvrow[:], start=True, stop=True).then_inc(pe_s, 1)
                n += 1
                tn.wait_ge(vs, v0 + 7)
                groups = ([("s1", VTsb["q"], xq, c) for c in range(NQC)]
                          + [("s1", VTsb["k"], xk, c) for c in range(16)]
                          + [("s1", VTsb["v"], xk, c) for c in range(16)]
                          + [("proj", UTsb["q"], tq, c) for c in range(NQC)]
                          + [("proj", UTsb["k"], tk, c) for c in range(16)]
                          + [("vblk", None, tv, kb) for kb in range(NKB)])
                for gi_, (kind, wt_, src_, c) in enumerate(groups):
                    if gi_ >= 2:
                        tn.wait_ge(vs, v0 + (n - pe0) - 1)
                    if kind == "vblk":
                        tn.matmul(p1_slice(gi_, "vblk"), tv[:, c * 128:(c + 1) * 128],
                                  UTsb["v"][:], start=True, stop=True).then_inc(pe_s, 1)
                    else:
                        tn.matmul(p1_slice(gi_, kind), wt_[:], src_[:, c * 512:(c + 1) * 512],
                                  start=True, stop=True).then_inc(pe_s, 1)
                    n += 1
                assert n - pe0 == P1_PE
                for qc in range(NQC):
                    nq = pe0 + P1_PE + qc * QC_PE
                    if qc > 0 or hd > 0:
                        tn.wait_ge(vs, v0 + P1_V + qc * QC_V)
                    for g_ in range(NG + 1):
                        par = (g_ % 2) * 1024
                        if g_ < NG:
                            if g_ >= 2:
                                tn.wait_ge(ss, hd * HD_S + qc * QC_S + g_ - 1)
                            for j in range(2):
                                kb = 2 * g_ + j
                                mm = tn.matmul(pSd[:, par + j * 512:par + (j + 1) * 512],
                                               kfm[:, kb * 128:(kb + 1) * 128],
                                               qfm[:, qc * 512:(qc + 1) * 512],
                                               start=True, stop=True)
                            mm.then_inc(pe_s, 1)
                        if g_ >= 1:
                            gp = g_ - 1
                            tn.wait_ge(ss, hd * HD_S + qc * QC_S + gp + 1)
                            for j in range(2):
                                kb = 2 * gp + j
                                mm = tn.matmul(pO[:], vsb[:, kb * 128:(kb + 1) * 128],
                                               Pt[gp % 2][:, j * 512:(j + 1) * 512],
                                               start=(kb == 0), stop=(kb == NKB - 1))
                            mm.then_inc(pe_s, 1)
                            for j in range(2):
                                kb = 2 * gp + j
                                mm = tn.matmul(pL[0:1, :], onecol[:],
                                               Pt[gp % 2][:, j * 512:(j + 1) * 512],
                                               start=(kb == 0), stop=(kb == NKB - 1))
                            mm.then_inc(pe_s, 1)
                    tn.wait_ge(vs, v0 + P1_V + qc * QC_V + 2)
                    tn.matmul(pL[:], onerow[:], rl[:], start=True, stop=True).then_inc(pe_s, 1)

        @block.vector
        def _(v):
            V = Eng(v, vs)
            for hd in range(8):
                pe0 = hd * HD_PE
                n = 0
                for nm in "qkv":
                    n += 1
                    v.wait_ge(pe_s, pe0 + n)
                    V.do(v.tensor_copy(VTsb[nm][:], pT[:, 0:32]))
                    n += 1
                    v.wait_ge(pe_s, pe0 + n)
                    V.do(v.tensor_copy(UTsb[nm][:], pT[0:32, 0:128]))
                n += 1
                v.wait_ge(pe_s, pe0 + n)
                V.do(v.tensor_copy(bvb[:], pL[:, 0:128]))
                evacs = ([("s1", tq, c, 512) for c in range(NQC)]
                         + [("s1", tk, c, 512) for c in range(16)]
                         + [("s1", tv, c, 512) for c in range(16)]
                         + [("projq", qfm, c, 512) for c in range(NQC)]
                         + [("projk", kfm, c, 512) for c in range(16)]
                         + [("vblk", vsb, kb, 128) for kb in range(NKB)])
                for gi_, (kind, buf, c, w) in enumerate(evacs):
                    n += 1
                    v.wait_ge(pe_s, pe0 + n)
                    if kind == "s1":
                        V.do(v.tensor_copy(buf[:, c * 512:(c + 1) * 512],
                                           p1_slice(gi_, "s1")))
                    elif kind == "projq":
                        V.do(v.tensor_scalar(buf[:, c * 512:(c + 1) * 512],
                                             p1_slice(gi_, "proj"),
                                             bsb["q"][:, hd:hd + 1], None, ALU.add))
                    elif kind == "projk":
                        V.do(v.tensor_scalar(buf[:, c * 512:(c + 1) * 512],
                                             p1_slice(gi_, "proj"),
                                             bsb["k"][:, hd:hd + 1], None, ALU.add))
                    else:
                        V.do(v.tensor_tensor(buf[:, c * 128:(c + 1) * 128],
                                             p1_slice(gi_, "vblk"),
                                             bvb[:, 0:128], ALU.add))
                assert n == P1_PE
                for qc in range(NQC):
                    nq = pe0 + P1_PE + qc * QC_PE
                    v.wait_ge(pe_s, nq + 3 * NG)
                    V.do(v.tensor_copy(rl[:], pL[0:1, :]))
                    V.do(v.reciprocal(rl[:], rl[:]))
                    v.wait_ge(pe_s, nq + 3 * NG + 1)
                    V.do(v.tensor_copy(rlb[:], pL[:]))
                    if qc > 0 or hd > 0:
                        v.wait_ge(st_s, 16 * (hd * NQC + qc))
                    V.do(v.tensor_mul(on[:], pO[:], rlb[:]))

        @block.scalar
        def _(sc):
            for hd in range(8):
                pe0 = hd * HD_PE
                for qc in range(NQC):
                    nq = pe0 + P1_PE + qc * QC_PE
                    for g_ in range(NG):
                        sc.wait_ge(pe_s, nq + (1 if g_ == 0 else 3 * g_ - 1))
                        if g_ >= 2:
                            sc.wait_ge(pe_s, nq + 3 * g_ - 2)
                        sc.activation(out=Pt[g_ % 2][:], in_=pSd[:, (g_ % 2) * 1024:(g_ % 2) * 1024 + 1024],
                                      func=AF.Exp, scale=SCL).then_inc(ss, 1)
                    sc.wait_ge(vs, hd * HD_V + P1_V + (qc + 1) * QC_V)
                    sc.dma_start(out=so[hd * 128:(hd + 1) * 128, qc * 512:(qc + 1) * 512],
                                 in_=on[:]).then_inc(st_s, 16)
            sc.wait_ge(st_s, 16 * 8 * NQC)
    return nc


def build_combine():
    NT = TPC // 128
    YR = 65664
    nc = bass.Bass("TRN2", target_bir_lowering=False)
    Y = nc.dram_tensor("Y", [YR, D], BF16, kind="ExternalInput").ap()
    sl1 = nc.dram_tensor("slot1", [128, NT], I32, kind="ExternalInput").ap()
    sl2 = nc.dram_tensor("slot2", [128, NT], I32, kind="ExternalInput").ap()
    w1h = nc.dram_tensor("w1h", [128, NT], F32, kind="ExternalInput").ap()
    w2h = nc.dram_tensor("w2h", [128, NT], F32, kind="ExternalInput").ap()
    bias = nc.dram_tensor("bias", [1, D], F32, kind="ExternalInput").ap()
    s_in = nc.dram_tensor("s", [D, TPC], BF16, kind="ExternalInput").ap()
    x_in = nc.dram_tensor("x", [TPC, D], F32, kind="ExternalInput").ap()
    out1 = nc.dram_tensor("out1", [TPC, D], F32, kind="ExternalOutput").ap()
    xn1 = nc.dram_tensor("xn1", [TPC, D], F32, kind="ExternalOutput").ap()
    with ExitStack() as ctx:
        sb = lambda n, s, d=F32: ctx.enter_context(nc.sbuf_tensor(n, s, d))
        psum = lambda n, s, d=F32: ctx.enter_context(nc.psum_tensor(n, s, d))
        sem = lambda n: ctx.enter_context(nc.semaphore(n))
        slt = [sb(f"slt{r}", [128, NT], I32) for r in range(2)]
        wt = [sb(f"wt{r}", [128, NT]) for r in range(2)]
        brow = sb("brow", [1, D])
        onerow = sb("onerow", [1, 128])
        bbc = sb("bbc", [128, D])
        y1 = sb("y1", [128, D], BF16)
        y2 = sb("y2", [128, D], BF16)
        st_ = sb("st", [128, D], BF16)
        xt = sb("xt", [128, D])
        gt = sb("gt", [128, D])
        gg = sb("gg", [128, D])
        o1 = sb("o1", [128, D])
        stats = sb("stats", [128, 12])
        mv = sb("mv", [128, 2])
        rstd = sb("rstd", [128, 1])
        xnt = sb("xnt", [128, D])
        pB = psum("pB", [128, 512])
        ld_s = sem("ld"); gd_s = sem("gd"); vs = sem("vs"); ss = sem("ss")
        pe_s = sem("pe"); st_s = sem("st"); gs = sem("gs")
        block = ctx.enter_context(nc.Block())
        NV = 13
        NS = 2

        @block.sync
        def _(sync):
            sync.dma_start(out=slt[0][:], in_=sl1[:]).then_inc(ld_s, 16)
            sync.dma_start(out=slt[1][:], in_=sl2[:]).then_inc(ld_s, 16)
            sync.dma_start(out=wt[0][:], in_=w1h[:]).then_inc(ld_s, 16)
            sync.dma_start(out=wt[1][:], in_=w2h[:]).then_inc(ld_s, 16)
            sync.dma_start(out=brow[:], in_=bias[:]).then_inc(ld_s, 16)
            for i in range(NT):
                if i >= 1:
                    sync.wait_ge(vs, 2 + NV * (i - 1) + 6)
                sync.dma_start(out=st_[:], in_=s_in[:, i * 128:(i + 1) * 128],
                               transpose=True).then_inc(ld_s, 16)
                sync.dma_start(out=xt[:], in_=x_in[i * 128:(i + 1) * 128, :]).then_inc(ld_s, 16)

        @block.gpsimd
        def _(g):
            G = Eng(g, gs)
            G.do(g.memset(onerow[:], 1.0))
            g.wait_ge(ld_s, 16 * 5)
            for i in range(NT):
                if i >= 1:
                    g.wait_ge(vs, 2 + NV * (i - 1) + 2)
                g.indirect_dma_start(out=y1[:], out_offset=None, in_=Y,
                                     in_offset=IOX(ap=slt[0][:, i:i + 1], axis=0)).then_inc(gd_s, 16)
                g.indirect_dma_start(out=y2[:], out_offset=None, in_=Y,
                                     in_offset=IOX(ap=slt[1][:, i:i + 1], axis=0)).then_inc(gd_s, 16)

        @block.tensor
        def _(tn):
            tn.wait_ge(gs, 1)
            tn.wait_ge(ld_s, 16 * 5)
            for half in range(2):
                tn.matmul(pB[:], onerow[:], brow[:, half * 512:(half + 1) * 512],
                          start=True, stop=True).then_inc(pe_s, 1)
                tn.wait_ge(vs, half + 1)

        @block.vector
        def _(v):
            V = Eng(v, vs)
            for half in range(2):
                v.wait_ge(pe_s, half + 1)
                V.do(v.tensor_copy(bbc[:, half * 512:(half + 1) * 512], pB[:]))
            for i in range(NT):
                v.wait_ge(gd_s, 32 * (i + 1))
                v.wait_ge(ld_s, 16 * 5 + 32 * (i + 1))
                V.do(v.tensor_scalar(gt[:], y1[:], wt[0][:, i:i + 1], None, ALU.mult))
                V.do(v.scalar_tensor_tensor(gg[:], y2[:], 1.0, gt[:], ALU.mult, ALU.bypass)) \
                    if False else \
                    V.do(v.tensor_scalar(gg[:], y2[:], wt[1][:, i:i + 1], None, ALU.mult))
                V.do(v.tensor_add(gt[:], gt[:], gg[:]))
                V.do(v.tensor_add(gt[:], gt[:], bbc[:]))
                v.wait_ge(ss, NS * i + 1)
                V.do(v.tensor_mul(gt[:], st_[:], gg[:]))
                if i >= 1:
                    v.wait_ge(st_s, 32 * (i - 1) + 16)
                V.do(v.tensor_add(o1[:], xt[:], gt[:]))
                V.do(v.bn_stats(stats[:, 0:6], o1[:, 0:512]))
                V.do(v.bn_stats(stats[:, 6:12], o1[:, 512:D]))
                V.do(v.bn_aggr(mv[:, 0:2], stats[:, 0:12]))
                V.do(v.tensor_scalar(mv[:, 1:2], mv[:, 1:2], 1e-5, None, ALU.add))
                v.wait_ge(ss, NS * i + 2)
                V.do(v.reciprocal(rstd[:, 0:1], rstd[:, 0:1]))
                if i >= 1:
                    v.wait_ge(st_s, 32 * i)
                V.do(v.tensor_scalar(xnt[:], o1[:], mv[:, 0:1], rstd[:, 0:1],
                                     ALU.subtract, ALU.mult))
                V.do(v.tensor_copy(gg[:], gt[:]))
                assert V.n == 2 + NV * (i + 1)

        @block.scalar
        def _(sc):
            SC = Eng(sc, ss)
            for i in range(NT):
                sc.wait_ge(vs, 2 + NV * i + 4)
                SC.do(sc.activation(out=gg[:], in_=gt[:], func=AF.Silu))
                sc.wait_ge(vs, 2 + NV * i + 10)
                SC.do(sc.activation(out=rstd[:, 0:1], in_=mv[:, 1:2], func=AF.Sqrt))
                sc.wait_ge(vs, 2 + NV * i + 12)
                sc.dma_start(out=out1[i * 128:(i + 1) * 128, :], in_=o1[:]).then_inc(st_s, 16)
                sc.dma_start(out=xn1[i * 128:(i + 1) * 128, :], in_=xnt[:]).then_inc(st_s, 16)
            sc.wait_ge(st_s, 32 * NT)
    return nc


def build_l1prep():
    R, C = 4096, D
    NTt = R // 128
    nc = bass.Bass("TRN2", target_bir_lowering=False)
    wf = nc.dram_tensor("wf", [R, C], F32, kind="ExternalInput").ap()
    wt_o = nc.dram_tensor("wt", [R, C], BF16, kind="ExternalOutput").ap()
    with ExitStack() as ctx:
        sb = lambda n, s, d=F32: ctx.enter_context(nc.sbuf_tensor(n, s, d))
        psum = lambda n, s, d=F32: ctx.enter_context(nc.psum_tensor(n, s, d))
        sem = lambda n: ctx.enter_context(nc.semaphore(n))
        wtile = sb("wtile", [128, 2 * C])
        acc = sb("acc", [128, 1])
        red = sb("red", [128, 1])
        onescol = sb("onescol", [128, 1])
        onerow = sb("onerow", [1, 128])
        thr1 = sb("thr1", [1, 1])
        thrc = sb("thrc", [128, 1])
        aa = sb("aa", [128, C])
        mm_ = sb("mm", [128, C])
        sgn = sb("sgn", [128, C])
        ot = sb("ot", [128, 2 * C], BF16)
        pS = psum("pS", [1, 1])
        pC = psum("pC", [128, 1])
        ld_s = sem("ld"); vs = sem("vs"); ss = sem("ss"); gs = sem("gs")
        pe_s = sem("pe"); st_s = sem("st")
        block = ctx.enter_context(nc.Block())
        BASE2 = 2 + 2 * NTt + 3

        @block.sync
        def _(sync):
            for i in range(NTt):
                b = (i % 2) * C
                if i >= 2:
                    sync.wait_ge(vs, 2 + 2 * (i - 1))
                sync.dma_start(out=wtile[:, b:b + C],
                               in_=wf[i * 128:(i + 1) * 128, :]).then_inc(ld_s, 16)
            for i in range(NTt):
                b = (i % 2) * C
                if i < 2:
                    sync.wait_ge(vs, 2 + 2 * NTt)
                else:
                    sync.wait_ge(vs, BASE2 + 3 * (i - 1))
                    sync.wait_ge(ss, 2 * (i - 1))
                sync.dma_start(out=wtile[:, b:b + C],
                               in_=wf[i * 128:(i + 1) * 128, :]).then_inc(ld_s, 16)

        @block.vector
        def _(v):
            V = Eng(v, vs)
            V.do(v.memset(acc[:], 0.0))
            V.do(v.memset(onerow[:], 1.0))
            for i in range(NTt):
                b = (i % 2) * C
                v.wait_ge(ld_s, 16 * (i + 1))
                V.do(v.tensor_reduce(red[:], wtile[:, b:b + C], AX.X, ALU.add,
                                     apply_absolute_value=True))
                V.do(v.tensor_add(acc[:], acc[:], red[:]))
            v.wait_ge(pe_s, 1)
            V.do(v.tensor_copy(thr1[:], pS[:]))
            v.wait_ge(pe_s, 2)
            V.do(v.tensor_copy(thrc[:], pC[:]))
            V.do(v.tensor_scalar(thrc[:], thrc[:], 0.5 / (R * C), None, ALU.mult))
            assert V.n == BASE2
            for i in range(NTt):
                b = (i % 2) * C
                v.wait_ge(ss, 2 * i + 1)
                V.do(v.tensor_scalar(mm_[:], aa[:], thrc[:, 0:1], None, ALU.is_gt))
                v.wait_ge(ss, 2 * i + 2)
                if i >= 2:
                    v.wait_ge(st_s, 16 * (i - 1))
                V.do(v.tensor_tensor(ot[:, (i % 2) * C:(i % 2) * C + C], sgn[:], mm_[:],
                                     ALU.mult))
                V.do(v.tensor_copy(red[:], acc[:]))

        @block.scalar
        def _(sc):
            SC = Eng(sc, ss)
            for i in range(NTt):
                b = (i % 2) * C
                sc.wait_ge(ld_s, 16 * (NTt + i + 1))
                if i >= 1:
                    sc.wait_ge(vs, BASE2 + 3 * (i - 1) + 1)
                SC.do(sc.activation(out=aa[:], in_=wtile[:, b:b + C], func=AF.Abs))
                if i >= 1:
                    sc.wait_ge(vs, BASE2 + 3 * (i - 1) + 2)
                SC.do(sc.activation(out=sgn[:], in_=wtile[:, b:b + C], func=AF.Sign))

        @block.tensor
        def _(tn):
            tn.wait_ge(vs, 2 + 2 * NTt)
            tn.wait_ge(gs, 1)
            tn.matmul(pS[:], acc[:], onescol[:], start=True, stop=True).then_inc(pe_s, 1)
            tn.wait_ge(vs, 2 + 2 * NTt + 1)
            tn.matmul(pC[:], onerow[:], thr1[:], start=True, stop=True).then_inc(pe_s, 1)

        @block.gpsimd
        def _(g):
            g.memset(onescol[:], 1.0).then_inc(gs, 1)
            for i in range(NTt):
                g.wait_ge(vs, BASE2 + 3 * i + 2)
                g.dma_start(out=wt_o[i * 128:(i + 1) * 128, :],
                            in_=ot[:, (i % 2) * C:(i % 2) * C + C]).then_inc(st_s, 16)
            g.wait_ge(st_s, 16 * NTt)
    return nc


def build_l1ffn():
    FF = 4096
    NC_ = TPC // 512
    nc = bass.Bass("TRN2", target_bir_lowering=False)
    xn1T = nc.dram_tensor("xn1T", [D, TPC], F32, kind="ExternalInput").ap()
    wupT = nc.dram_tensor("wupT", [D, FF], BF16, kind="ExternalInput").ap()
    wdnT = nc.dram_tensor("wdnT", [FF, D], BF16, kind="ExternalInput").ap()
    s1pa = nc.dram_tensor("s1pa", [TPC, D], F32, kind="ExternalInput").ap()
    s1pb = nc.dram_tensor("s1pb", [TPC, D], F32, kind="ExternalInput").ap()
    out1 = nc.dram_tensor("out1", [TPC, D], F32, kind="ExternalInput").ap()
    oo = nc.dram_tensor("out", [TPC, D], F32, kind="ExternalOutput").ap()
    with ExitStack() as ctx:
        sb = lambda n, s, d=F32: ctx.enter_context(nc.sbuf_tensor(n, s, d))
        psum = lambda n, s, d=F32: ctx.enter_context(nc.psum_tensor(n, s, d))
        sem = lambda n: ctx.enter_context(nc.semaphore(n))
        wup_sb = sb("wup", [128, 8 * FF], BF16)
        wdn_sb = sb("wdn", [128, 32 * D], BF16)
        xcb = sb("xcb", [128, 8 * 512], BF16)
        sq_b = sb("sq_b", [128, 8 * 512], BF16)
        onescol = sb("onescol", [128, 1])
        onescolb = sb("onescolb", [128, 1], BF16)
        onerow = sb("onerow", [1, 128])
        one11 = sb("one11", [1, 1])
        rr = sb("rr", [1, 512])
        r1b = sb("r1b", [128, 512])
        hs = sb("hs", [128, 512])
        hg = sb("hg", [128, 512])
        hq2 = sb("hq2", [128, 512])
        hfull = sb("hfull", [128, 32 * 512], BF16)
        r2 = sb("r2", [1, 512])
        r2c = sb("r2c", [128, 1])
        gsv = sb("gsv", [128, 1024])
        pa_t = sb("pa_t", [128, D])
        pb_t = sb("pb_t", [128, D])
        o1_t = sb("o1_t", [128, D])
        pR1 = psum("pR1", [1, 512])
        pB = psum("pB", [128, 512])
        pH = psum("pH", [128, 2 * 512])
        pR2 = psum("pR2", [1, 512])
        pC = psum("pC", [128, 1])
        pG = psum("pG", [128, 2 * 512])
        ld_s = sem("ld"); gd_s = sem("gd"); sd_s = sem("sd")
        vs = sem("vs"); ss = sem("ss"); pe_s = sem("pe"); st_s = sem("st")
        gs = sem("gs")
        block = ctx.enter_context(nc.Block())
        V_C = 140
        S_C = 42
        PEC = 2 + 64 + 12

        @block.gpsimd
        def _(g):
            G = Eng(g, gs)
            G.do(g.memset(onescol[:], 1.0))
            G.do(g.memset(onescolb[:], 1.0))
            G.do(g.memset(onerow[:], 1.0))
            G.do(g.memset(one11[:], 1.0))
            g.dma_start(out=wup_sb[:].rearrange("p (c f) -> p c f", f=FF),
                        in_=wupT.rearrange("(c p) f -> p c f", p=128)).then_inc(gd_s, 16)
            g.dma_start(out=wdn_sb[:].rearrange("p (c o) -> p c o", o=D),
                        in_=wdnT.rearrange("(c p) o -> p c o", p=128)).then_inc(gd_s, 16)
            for i in range(NC_):
                if i >= 1:
                    g.wait_ge(pe_s, PEC * (i - 1) + 2 + 64)
                    g.wait_ge(vs, V_C * (i - 1) + 1)
                g.dma_start(out=xcb[:].rearrange("p (c t) -> p c t", t=512),
                            in_=xn1T[:, i * 512:(i + 1) * 512]
                            .rearrange("(c p) t -> p c t", p=128)).then_inc(gd_s, 16)

        @block.tensor
        def _(tn):
            tn.wait_ge(gs, 4)
            tn.wait_ge(gd_s, 32)
            for i in range(NC_):
                p0 = PEC * i
                v0 = V_C * i
                n = 0
                tn.wait_ge(vs, v0 + 1)
                for dc in range(8):
                    mm = tn.matmul(pR1[:], onescolb[:], sq_b[:, dc * 512:(dc + 1) * 512],
                                   start=(dc == 0), stop=(dc == 7))
                mm.then_inc(pe_s, 1)
                n += 1
                tn.wait_ge(vs, v0 + 4)
                tn.matmul(pB[:], onerow[:], rr[:], start=True, stop=True).then_inc(pe_s, 1)
                n += 1
                tn.wait_ge(gd_s, 32 + 16 * (i + 1))
                for fc in range(32):
                    hb = (fc % 2) * 512
                    if i * 32 + fc >= 2:
                        pfc, pgi = fc - 2, i
                        if pfc < 0:
                            pfc += 32
                            pgi -= 1
                        tn.wait_ge(vs, V_C * pgi + 5 + 3 * pfc + 3)
                    for dc in range(8):
                        mm = tn.matmul(pH[:, hb:hb + 512],
                                       wup_sb[:, dc * FF + fc * 128:dc * FF + (fc + 1) * 128],
                                       xcb[:, dc * 512:(dc + 1) * 512],
                                       start=(dc == 0), stop=(dc == 7))
                    mm.then_inc(pe_s, 1)
                    n += 1
                    tn.wait_ge(vs, v0 + 5 + 3 * fc + 3)
                    tn.matmul(pR2[:], onescol[:], hq2[:],
                              start=(fc == 0), stop=(fc == 31)).then_inc(pe_s, 1)
                    n += 1
                for sub in range(4):
                    tn.wait_ge(vs, v0 + 104)
                    tn.matmul(pC[:], r2[0:1, sub * 128:(sub + 1) * 128], one11[:],
                              start=True, stop=True).then_inc(pe_s, 1)
                    n += 1
                    if i + sub > 0:
                        psub, pgi = sub - 1, i
                        if psub < 0:
                            psub += 4
                            pgi -= 1
                        tn.wait_ge(vs, V_C * pgi + 104 + 9 * psub + 7)
                    for half in range(2):
                        for fc in range(32):
                            mm = tn.matmul(pG[:, half * 512:(half + 1) * 512],
                                           hfull[:, fc * 512 + sub * 128:fc * 512 + sub * 128 + 128],
                                           wdn_sb[:, fc * D + half * 512:fc * D + (half + 1) * 512],
                                           start=(fc == 0), stop=(fc == 31))
                        mm.then_inc(pe_s, 1)
                        n += 1
                assert n == PEC, n

        @block.vector
        def _(v):
            V = Eng(v, vs)
            for i in range(NC_):
                p0 = PEC * i
                v0 = V_C * i
                v.wait_ge(gd_s, 32 + 16 * (i + 1))
                V.do(v.tensor_mul(sq_b[:], xcb[:], xcb[:]))
                v.wait_ge(pe_s, p0 + 1)
                V.do(v.tensor_copy(rr[:], pR1[:]))
                v.wait_ge(ss, S_C * i + 1)
                V.do(v.tensor_scalar(rr[:], rr[:], 1e-8, None, ALU.add))
                V.do(v.reciprocal(rr[:], rr[:]))
                v.wait_ge(pe_s, p0 + 2)
                V.do(v.tensor_copy(r1b[:], pB[:]))
                for fc in range(32):
                    hb = (fc % 2) * 512
                    v.wait_ge(pe_s, p0 + 2 + 2 * fc + 1)
                    V.do(v.tensor_tensor(hs[:], pH[:, hb:hb + 512], r1b[:], ALU.mult))
                    v.wait_ge(ss, S_C * i + 1 + fc + 1)
                    V.do(v.tensor_copy(hfull[:, fc * 512:(fc + 1) * 512], hg[:]))
                    V.do(v.tensor_mul(hq2[:], hg[:], hg[:]))
                v.wait_ge(pe_s, p0 + 2 + 64)
                V.do(v.tensor_copy(r2[:], pR2[:]))
                v.wait_ge(ss, S_C * i + 34)
                V.do(v.tensor_scalar(r2[:], r2[:], 1e-8, None, ALU.add))
                V.do(v.reciprocal(r2[:], r2[:]))
                assert V.n == v0 + 104
                for sub in range(4):
                    v.wait_ge(pe_s, p0 + 66 + 3 * sub + 1)
                    V.do(v.tensor_copy(r2c[:], pC[:]))
                    v.wait_ge(sd_s, 48 * (4 * i + sub + 1))
                    V.do(v.tensor_add(pa_t[:], pa_t[:], pb_t[:]))
                    for half in range(2):
                        v.wait_ge(pe_s, p0 + 66 + 3 * sub + 2 + half)
                        V.do(v.tensor_scalar(gsv[:, half * 512:(half + 1) * 512],
                                             pG[:, half * 512:(half + 1) * 512],
                                             r2c[:, 0:1], None, ALU.mult))
                    v.wait_ge(ss, S_C * i + 34 + 2 * (sub + 1))
                    V.do(v.tensor_mul(gsv[:], pa_t[:], gsv[:]))
                    if 4 * i + sub >= 2:
                        v.wait_ge(st_s, 16 * (4 * i + sub - 1))
                    V.do(v.tensor_add(o1_t[:], o1_t[:], gsv[:]))
                    V.do(v.tensor_copy(hq2[:, 0:1], hq2[:, 0:1]))
                    V.do(v.tensor_copy(hq2[:, 0:1], hq2[:, 0:1]))
                    V.do(v.tensor_copy(hq2[:, 0:1], hq2[:, 0:1]))

        @block.scalar
        def _(sc):
            SC = Eng(sc, ss)
            for i in range(NC_):
                v0 = V_C * i
                sc.wait_ge(vs, v0 + 2)
                SC.do(sc.activation(out=rr[:], in_=rr[:], func=AF.Sqrt, scale=1.0 / D))
                for fc in range(32):
                    sc.wait_ge(vs, v0 + 5 + 3 * fc + 1)
                    SC.do(sc.activation(out=hg[:], in_=hs[:], func=AF.Gelu))
                sc.wait_ge(vs, v0 + 102)
                SC.do(sc.activation(out=r2[:], in_=r2[:], func=AF.Sqrt, scale=1.0 / FF))
                for sub in range(4):
                    r0 = i * 512 + sub * 128
                    k = 4 * i + sub
                    if k >= 1:
                        pv0 = V_C * i if sub >= 1 else V_C * (i - 1)
                        psub = sub - 1 if sub >= 1 else 3
                        sc.wait_ge(vs, pv0 + 104 + 9 * psub + 6)
                    sc.dma_start(out=pa_t[:], in_=s1pa[r0:r0 + 128, :]).then_inc(sd_s, 16)
                    sc.dma_start(out=pb_t[:], in_=s1pb[r0:r0 + 128, :]).then_inc(sd_s, 16)
                    sc.dma_start(out=o1_t[:], in_=out1[r0:r0 + 128, :]).then_inc(sd_s, 16)
                    sc.wait_ge(vs, v0 + 104 + 9 * sub + 4)
                    SC.do(sc.activation(out=gsv[:], in_=gsv[:], func=AF.Silu))
                    SC.do(sc.activation(out=hg[:, 0:1], in_=hg[:, 0:1], func=AF.Sqrt))
                    sc.wait_ge(vs, v0 + 104 + 9 * sub + 6)
                    sc.dma_start(out=oo[r0:r0 + 128, :], in_=o1_t[:]).then_inc(st_s, 16)
            sc.wait_ge(st_s, 16 * 4 * NC_)
    return nc


class Prog:

    def __init__(self):
        self.steps = []
        self.last_w = {}
        self.readers = {}

    def add(self, eng, fn, reads=(), writes=(), inc=1, extra=()):
        sid = len(self.steps)
        after = set(extra)
        for b in reads:
            if b in self.last_w:
                after.add(self.last_w[b])
        for b in writes:
            for rr_ in self.readers.get(b, ()):
                after.add(rr_)
            if b in self.last_w:
                after.add(self.last_w[b])
        for b in reads:
            self.readers.setdefault(b, set()).add(sid)
        for b in writes:
            self.last_w[b] = sid
            self.readers[b] = set()
        self.steps.append((eng, fn, after, inc))
        return sid

    def finalize(self):
        cnt = {}
        self.pos = {}
        for sid, (eng, _, _, inc) in enumerate(self.steps):
            cnt[eng] = cnt.get(eng, 0) + inc
            self.pos[sid] = (eng, cnt[eng])
        self.total = cnt

    def emit(self, eng_names, raw, sems):
        if isinstance(eng_names, str):
            eng_names = (eng_names,)
        names = set(eng_names)
        n = {e: 0 for e in names}
        for sid, (eng, fn, after, inc) in enumerate(self.steps):
            if eng not in names:
                continue
            waits = {}
            for dep in sorted(after):
                deng, dcnt = self.pos[dep]
                if deng not in names:
                    waits[deng] = max(waits.get(deng, 0), dcnt)
                elif deng != eng:
                    waits[deng] = max(waits.get(deng, 0), dcnt)
            for deng, dcnt in waits.items():
                raw.wait_ge(sems[deng], dcnt)
            inst = fn(raw)
            n[eng] += inc
            inst.then_inc(sems[eng], inc)
            raw.wait_ge(sems[eng], n[eng])


def build_l1mix():
    CH = 2064
    NCH = 17
    NCK = 16
    TWO_PI = 2.0 * math.pi
    MC = 1.5 * (2 ** 23)
    nc = bass.Bass("TRN2", target_bir_lowering=False)
    xn1T = nc.dram_tensor("xn1T", [D, S], F32, kind="ExternalInput").ap()
    winb = nc.dram_tensor("winb", [D, CH], F32, kind="ExternalInput").ap()
    ckt = nc.dram_tensor("ckt", [128, 4 * NCH], F32, kind="ExternalInput").ap()
    cbt = nc.dram_tensor("cbt", [128, NCH], F32, kind="ExternalInput").ap()
    thc = nc.dram_tensor("thc", [128, 8], F32, kind="ExternalInput").ap()
    slope = nc.dram_tensor("slope", [16, 1], F32, kind="ExternalInput").ap()
    isanch = nc.dram_tensor("isanch", [16, 1], F32, kind="ExternalInput").ap()
    sscale = nc.dram_tensor("sscale", [16, 1], F32, kind="ExternalInput").ap()
    dlog = nc.dram_tensor("dlog", [1, 3], F32, kind="ExternalInput").ap()
    wre_i = nc.dram_tensor("wre", [64, 64], F32, kind="ExternalInput").ap()
    wim_i = nc.dram_tensor("wim", [64, 64], F32, kind="ExternalInput").ap()
    nsc0 = nc.dram_tensor("nsc0", [64, 1], F32, kind="ExternalInput").ap()
    nsc1 = nc.dram_tensor("nsc1", [64, 1], F32, kind="ExternalInput").ap()
    wout = nc.dram_tensor("wout", [1024, 1024], F32, kind="ExternalInput").ap()
    E1_i = nc.dram_tensor("E1", [16, 1024], F32, kind="ExternalInput").ap()
    E2_i = nc.dram_tensor("E2", [128, 128], F32, kind="ExternalInput").ap()
    s1p = nc.dram_tensor("s1p", [S, D], F32, kind="ExternalOutput").ap()
    with ExitStack() as ctx:
        sb = lambda n, s, d=F32: ctx.enter_context(nc.sbuf_tensor(n, s, d))
        psum = lambda n, s, d=F32: ctx.enter_context(nc.psum_tensor(n, s, d))
        sem = lambda n: ctx.enter_context(nc.semaphore(n))
        win_sb = sb("win_sb", [128, 8 * CH], BF16)
        wout_sb = sb("wout_sb", [128, 8 * D], BF16)
        xcb = sb("xcb", [128, 8 * 512], BF16)
        zwin = sb("zwin", [128, NCH * 516])
        xvb = sb("xvb", [128, 8 * 512])
        gateb = sb("gateb", [128, 8 * 512], BF16)
        ogb = sb("ogb", [128, 8 * 512], BF16)
        nr = sb("nr", [128, 8 * 512], BF16)
        ni = sb("ni", [128, 8 * 512], BF16)
        pwbb = sb("pwbb", [128, 512], BF16)
        sinb = sb("sinb", [128, 512], BF16)
        cosb = sb("cosb", [128, 512], BF16)
        ut = sb("ut", [128, 512])
        r1t = sb("r1t", [128, 512])
        w1tt = sb("w1tt", [128, 512])
        xsqt = sb("xsqt", [128, 512])
        t1t = sb("t1t", [128, 512])
        polyt = sb("polyt", [128, 512])
        remt = sb("remt", [128, 512], BF16)
        immt = sb("immt", [128, 512], BF16)
        sq1 = sb("sq1", [128, 512])
        sq2 = sb("sq2", [128, 512])
        tmp = sb("tmp", [128, 512])
        ytmp = sb("ytmp", [128, 512])
        cbsb = sb("cbsb", [128, 512])
        otile = sb("otile", [128, 2 * D])
        ones128b = sb("ones128b", [128, 512], BF16)
        ones16 = sb("ones16", [16, 512])
        post = sb("post", [16, 512])
        dist = sb("dist", [16, 512])
        twd = sb("twd", [16, 512])
        twa = sb("twa", [16, 512])
        tw = sb("tw", [16, 512])
        pt_ = sb("pt", [16, 512])
        pwt = sb("pwt", [16, 512])
        dnt = sb("dnt", [16, 512])
        invt = sb("invt", [16, 512])
        c_t = sb("c_t", [16, 512])
        sqs = sb("sqs", [16, 512])
        mst = sb("mst", [16, 512])
        ck_sb = sb("ck_sb", [128, 4 * NCH])
        cb_sb = sb("cb_sb", [128, NCH])
        thscl = sb("thscl", [128, 8])
        slp = sb("slp", [16, 1])
        nsp = sb("nsp", [16, 1])
        anch = sb("anch", [16, 1])
        sscl = sb("sscl", [16, 1])
        spt1 = sb("spt1", [16, 1])
        spt2 = sb("spt2", [16, 1])
        dl = sb("dl", [1, 3])
        dm = sb("dm", [1, 1])
        wsm = sb("wsm", [1, 3])
        onerow = sb("onerow", [1, 128])
        w0c = sb("w0c", [128, 1])
        w1c = sb("w1c", [128, 1])
        w2n = sb("w2n", [128, 1])
        wre_s = sb("wre_s", [64, 64])
        wim_s = sb("wim_s", [64, 64])
        wre_b = sb("wre_b", [128, 64], BF16)
        wim_b = sb("wim_b", [128, 64], BF16)
        ns0 = sb("ns0", [64, 1])
        ns1 = sb("ns1", [64, 1])
        E1_sb = sb("E1_sb", [16, 1024])
        E2_sb = sb("E2_sb", [128, 128])
        carryR = sb("carryR", [128, 8])
        carryI = sb("carryI", [128, 8])
        carryD = sb("carryD", [16, 1])
        pz = psum("pz", [128, 512])
        ppw = psum("ppw", [128, 512])
        psq = psum("psq", [16, 512])
        py = psum("py", [128, 512])
        pcb = psum("pcb", [128, 512])
        po = psum("po", [128, 2 * 512])
        pw0 = psum("pw0", [128, 1])
        sems = {e: sem(e) for e in ("vec", "sca", "ten", "gps", "syn", "scd")}
        P = Prog()

        def V(fn, reads=(), writes=()):
            return P.add("vec", fn, reads, writes)

        def SC(fn, reads=(), writes=()):
            return P.add("sca", fn, reads, writes)

        def TN(fn, reads=(), writes=()):
            return P.add("ten", fn, reads, writes)

        def GP(fn, reads=(), writes=()):
            return P.add("gps", fn, reads, writes)

        def LD(q, fn, reads=(), writes=()):
            return P.add(q, fn, reads, writes, inc=16)

        GP(lambda r: r.memset(ones128b[:], 1.0), writes=["ones128b"])
        GP(lambda r: r.memset(ones16[:], 1.0), writes=["ones16"])
        GP(lambda r: r.memset(onerow[:], 1.0), writes=["onerow"])
        GP(lambda r: r.memset(carryR[:], 0.0), writes=["carryR"])
        GP(lambda r: r.memset(carryI[:], 0.0), writes=["carryI"])
        GP(lambda r: r.memset(carryD[:], 0.0), writes=["carryD"])
        LD("gps", lambda r: r.dma_start(
            out=win_sb[:].rearrange("p (c f) -> p c f", f=CH),
            in_=winb.rearrange("(c p) f -> p c f", p=128)), writes=["win"])
        LD("gps", lambda r: r.dma_start(
            out=wout_sb[:].rearrange("p (c o) -> p c o", o=D),
            in_=wout.rearrange("(c p) o -> p c o", p=128)), writes=["wout"])
        LD("syn", lambda r: r.dma_start(out=ck_sb[:], in_=ckt[:]), writes=["ck"])
        LD("syn", lambda r: r.dma_start(out=cb_sb[:], in_=cbt[:]), writes=["cb"])
        LD("syn", lambda r: r.dma_start(out=thscl[:], in_=thc[:]), writes=["thscl"])
        LD("syn", lambda r: r.dma_start(out=slp[:], in_=slope[:]), writes=["slp"])
        LD("syn", lambda r: r.dma_start(out=anch[:], in_=isanch[:]), writes=["anch"])
        LD("syn", lambda r: r.dma_start(out=sscl[:], in_=sscale[:]), writes=["sscl"])
        LD("syn", lambda r: r.dma_start(out=dl[:], in_=dlog[:]), writes=["dl"])
        LD("syn", lambda r: r.dma_start(out=wre_s[:], in_=wre_i[:]), writes=["wre"])
        LD("syn", lambda r: r.dma_start(out=wim_s[:], in_=wim_i[:]), writes=["wim"])
        LD("syn", lambda r: r.dma_start(out=ns0[:], in_=nsc0[:]), writes=["ns0"])
        LD("syn", lambda r: r.dma_start(out=ns1[:], in_=nsc1[:]), writes=["ns1"])
        LD("syn", lambda r: r.dma_start(out=E1_sb[:], in_=E1_i[:]), writes=["E1"])
        LD("syn", lambda r: r.dma_start(out=E2_sb[:], in_=E2_i[:]), writes=["E2"])
        V(lambda r: r.tensor_scalar(spt1[:], slp[:], 0.0, None, ALU.max),
          reads=["slp"], writes=["spt1"])
        SC(lambda r: r.activation(out=spt2[:], in_=slp[:], func=AF.Abs),
           reads=["slp"], writes=["spt2"])
        SC(lambda r: r.activation(out=spt2[:], in_=spt2[:], func=AF.Exp, scale=-1.0),
           reads=["spt2"], writes=["spt2"])
        SC(lambda r: r.activation(out=spt2[:], in_=spt2[:], func=AF.Ln, bias=1.0),
           reads=["spt2"], writes=["spt2"])
        V(lambda r: r.tensor_add(spt1[:], spt1[:], spt2[:]),
          reads=["spt1", "spt2"], writes=["spt1"])
        V(lambda r: r.tensor_scalar(nsp[:], spt1[:], -1.0, None, ALU.mult),
          reads=["spt1"], writes=["nsp"])
        V(lambda r: r.reduce_max(dm[:], dl[:], AX.X), reads=["dl"], writes=["dm"])
        V(lambda r: r.tensor_scalar(dm[:], dm[:], -1.0, None, ALU.mult),
          reads=["dm"], writes=["dm"])
        SC(lambda r: r.activation(out=wsm[:], in_=dl[:], func=AF.Exp, bias=dm[:, 0:1]),
           reads=["dl", "dm"], writes=["wsm"])
        V(lambda r: r.reduce_sum(dm[:], wsm[:], AX.X), reads=["wsm"], writes=["dm"])
        V(lambda r: r.reciprocal(dm[:], dm[:]), reads=["dm"], writes=["dm"])
        V(lambda r: r.tensor_scalar(wsm[:], wsm[:], dm[:, 0:1], None, ALU.mult),
          reads=["wsm", "dm"], writes=["wsm"])
        for j, tgt, key in ((0, w0c, "w0c"), (1, w1c, "w1c"), (2, w2n, "w2n")):
            TN(lambda r, j=j: r.matmul(pw0[:], onerow[:], wsm[0:1, j:j + 1],
                                       start=True, stop=True),
               reads=["onerow", "wsm"], writes=["pw0"])
            V(lambda r, t=tgt: r.tensor_copy(t[:], pw0[:]), reads=["pw0"], writes=[key])
        V(lambda r: r.tensor_scalar(w2n[:], w2n[:], -1.0, None, ALU.mult),
          reads=["w2n"], writes=["w2n"])
        V(lambda r: r.tensor_scalar(thscl[:], thscl[:], 1.0 / TWO_PI, None, ALU.mult),
          reads=["thscl"], writes=["thscl"])
        V(lambda r: r.tensor_scalar(wre_s[:], wre_s[:], ns0[:, 0:1], None, ALU.mult),
          reads=["wre", "ns0"], writes=["wre"])
        V(lambda r: r.tensor_scalar(wim_s[:], wim_s[:], ns1[:, 0:1], None, ALU.mult),
          reads=["wim", "ns1"], writes=["wim"])
        V(lambda r: r.tensor_copy(wre_b[0:64, :], wre_s[:]), reads=["wre"], writes=["wreb"])
        V(lambda r: r.tensor_copy(wre_b[64:128, :], wre_s[:]), reads=["wre"], writes=["wreb"])
        V(lambda r: r.tensor_copy(wim_b[0:64, :], wim_s[:]), reads=["wim"], writes=["wimb"])
        V(lambda r: r.tensor_copy(wim_b[64:128, :], wim_s[:]), reads=["wim"], writes=["wimb"])

        for i in range(NCK):
            LD("gps", lambda r, i=i: r.dma_start(
                out=xcb[:].rearrange("p (c t) -> p c t", t=512),
                in_=xn1T[:, i * 512:(i + 1) * 512].rearrange("(c p) t -> p c t", p=128)),
                writes=["xcb"])
            GP(lambda r, i=i: r.iota(post[:], pattern=[[1, 512]], base=i * 512,
                                     channel_multiplier=0,
                                     allow_small_or_imprecise_dtypes=True), writes=["post"])
            V(lambda r: r.tensor_scalar(dist[:], post[:], -1.0, float(S - 1), ALU.mult, ALU.add),
              reads=["post"], writes=["dist"])
            V(lambda r: r.tensor_scalar(dist[:], dist[:], nsp[:, 0:1], None, ALU.mult),
              reads=["dist", "nsp"], writes=["dist"])
            SC(lambda r: r.activation(out=twd[:], in_=dist[:], func=AF.Exp),
               reads=["dist"], writes=["twd"])
            V(lambda r: r.tensor_scalar(twa[:], post[:], nsp[:, 0:1], None, ALU.mult),
              reads=["post", "nsp"], writes=["twa"])
            SC(lambda r: r.activation(out=twa[:], in_=twa[:], func=AF.Exp),
               reads=["twa"], writes=["twa"])
            V(lambda r: r.tensor_sub(twa[:], twa[:], twd[:]),
              reads=["twa", "twd"], writes=["twa"])
            V(lambda r: r.scalar_tensor_tensor(tw[:], twa[:], anch[:, 0:1], twd[:],
                                               ALU.mult, ALU.add),
              reads=["twa", "anch", "twd"], writes=["tw"])
            for cc in range(NCH):
                w = 128 if cc < 16 else 16
                zb = cc * 516
                zk = f"zwin{cc}"
                def zmm(r, cc=cc, w=w):
                    for dc in range(8):
                        mm = r.matmul(pz[0:w, :] if w < 128 else pz[:],
                                      win_sb[:, dc * CH + cc * 128:dc * CH + cc * 128 + w],
                                      xcb[:, dc * 512:(dc + 1) * 512],
                                      start=(dc == 0), stop=(dc == 7))
                    return mm
                TN(zmm, reads=["xcb", "win"], writes=["pz"])
                if i == 0:
                    V(lambda r, zb=zb: r.memset(zwin[:, zb:zb + 3], 0.0), writes=[zk])
                V(lambda r, zb=zb, w=w: r.tensor_copy(
                    zwin[0:w, zb + 3:zb + 515] if w < 128 else zwin[:, zb + 3:zb + 515],
                    pz[0:w, :] if w < 128 else pz[:]),
                  reads=["pz", zk], writes=[zk, zk + "body"])
                V(lambda r, zb=zb, cc=cc: r.tensor_scalar(
                    tmp[:], zwin[:, zb:zb + 512], ck_sb[:, 0 * NCH + cc:0 * NCH + cc + 1],
                    None, ALU.mult), reads=[zk, "ck"], writes=["tmp"])
                for j in range(1, 4):
                    V(lambda r, zb=zb, cc=cc, j=j: r.scalar_tensor_tensor(
                        tmp[:], zwin[:, zb + j:zb + j + 512],
                        ck_sb[:, j * NCH + cc:j * NCH + cc + 1], tmp[:], ALU.mult, ALU.add),
                      reads=[zk, "ck", "tmp"], writes=["tmp"])
                if cc < 8:
                    V(lambda r, cc=cc: r.tensor_scalar(
                        xvb[:, cc * 512:(cc + 1) * 512], tmp[:], cb_sb[:, cc:cc + 1],
                        None, ALU.add), reads=["tmp", "cb"], writes=[f"xvb{cc}"])
                elif cc < 16:
                    V(lambda r, cc=cc: r.tensor_scalar(tmp[:], tmp[:], cb_sb[:, cc:cc + 1],
                                                       None, ALU.add),
                      reads=["tmp", "cb"], writes=["tmp"])
                    SC(lambda r, cc=cc: r.activation(
                        out=gateb[:, (cc - 8) * 512:(cc - 7) * 512], in_=tmp[:], func=AF.Silu),
                       reads=["tmp"], writes=[f"gateb{cc - 8}"])
                else:
                    V(lambda r, cc=cc: r.tensor_scalar(tmp[:], tmp[:], cb_sb[:, cc:cc + 1],
                                                       None, ALU.add),
                      reads=["tmp", "cb"], writes=["tmp", "sraw"])
                GP(lambda r, zb=zb: r.tensor_copy(zwin[:, zb:zb + 3],
                                                  zwin[:, zb + 512:zb + 515]),
                   reads=[zk + "body"], writes=[zk])
            V(lambda r: r.tensor_scalar(pt_[:], tmp[0:16, :], sscl[:, 0:1], None, ALU.mult),
              reads=["sraw", "tmp", "sscl"], writes=["pt"])
            V(lambda r: r.tensor_scalar(pt_[:], pt_[:], 20.0, None, ALU.min),
              reads=["pt"], writes=["pt"])
            V(lambda r: r.tensor_scalar(pt_[:], pt_[:], -20.0, None, ALU.max),
              reads=["pt"], writes=["pt"])
            SC(lambda r: r.activation(out=pt_[:], in_=pt_[:], func=AF.Exp),
               reads=["pt"], writes=["pt"])
            V(lambda r: r.tensor_mul(pwt[:], pt_[:], tw[:]),
              reads=["pt", "tw"], writes=["pwt"])
            V(lambda r: r.tensor_tensor_scan(dnt[:], ones16[:], pwt[:], carryD[:, 0:1],
                                             ALU.mult, ALU.add),
              reads=["ones16", "pwt", "carryD"], writes=["dnt"])
            V(lambda r: r.tensor_copy(carryD[:], dnt[:, 511:512]),
              reads=["dnt"], writes=["carryD"])
            V(lambda r: r.tensor_scalar(dnt[:], dnt[:], 1e-4, None, ALU.max),
              reads=["dnt"], writes=["dnt"])
            V(lambda r: r.reciprocal(invt[:], dnt[:]), reads=["dnt"], writes=["invt"])
            for vc in range(8):
                TN(lambda r, vc=vc: r.matmul(ppw[:], E1_sb[:, vc * 128:(vc + 1) * 128], pwt[:],
                                             start=True, stop=True),
                   reads=["E1", "pwt"], writes=["ppw"])
                V(lambda r: r.tensor_copy(pwbb[:], ppw[:]), reads=["ppw"], writes=["pwbb"])
                V(lambda r, vc=vc: r.tensor_scalar(ut[:], xvb[:, vc * 512:(vc + 1) * 512],
                                                   thscl[:, vc:vc + 1], None, ALU.mult),
                  reads=[f"xvb{vc}", "thscl"], writes=["ut"])
                V(lambda r: r.tensor_scalar(r1t[:], ut[:], MC, -MC, ALU.add, ALU.add),
                  reads=["ut"], writes=["r1t"])
                V(lambda r: r.tensor_sub(w1tt[:], ut[:], r1t[:]),
                  reads=["ut", "r1t"], writes=["w1tt"])
                SC(lambda r: r.activation(out=sinb[:], in_=w1tt[:], func=AF.Sin, scale=TWO_PI),
                   reads=["w1tt"], writes=["sinb"])
                V(lambda r: r.tensor_scalar(r1t[:], ut[:], 0.25 + MC, -MC, ALU.add, ALU.add),
                  reads=["ut"], writes=["r1t"])
                V(lambda r: r.scalar_tensor_tensor(w1tt[:], ut[:], 0.25, r1t[:],
                                                   ALU.add, ALU.subtract),
                  reads=["ut", "r1t"], writes=["w1tt"])
                SC(lambda r: r.activation(out=cosb[:], in_=w1tt[:], func=AF.Sin, scale=TWO_PI),
                   reads=["w1tt"], writes=["cosb"])
                SC(lambda r, vc=vc: r.activation(out=xsqt[:], in_=xvb[:, vc * 512:(vc + 1) * 512],
                                                 func=AF.Square),
                   reads=[f"xvb{vc}"], writes=["xsqt"])
                V(lambda r: r.tensor_scalar(t1t[:], xsqt[:], w2n[:, 0:1], w0c[:, 0:1],
                                            ALU.mult, ALU.add),
                  reads=["xsqt", "w2n", "w0c"], writes=["t1t"])
                V(lambda r, vc=vc: r.scalar_tensor_tensor(
                    polyt[:], xvb[:, vc * 512:(vc + 1) * 512], w1c[:, 0:1], t1t[:],
                    ALU.mult, ALU.add), reads=[f"xvb{vc}", "w1c", "t1t"], writes=["polyt"])
                V(lambda r: r.tensor_mul(remt[:], polyt[:], cosb[:]),
                  reads=["polyt", "cosb"], writes=["remt"])
                V(lambda r: r.tensor_mul(immt[:], polyt[:], sinb[:]),
                  reads=["polyt", "sinb"], writes=["immt"])
                V(lambda r: r.tensor_mul(remt[:], remt[:], pwbb[:]),
                  reads=["remt", "pwbb"], writes=["remt"])
                V(lambda r: r.tensor_mul(immt[:], immt[:], pwbb[:]),
                  reads=["immt", "pwbb"], writes=["immt"])
                V(lambda r, vc=vc: r.tensor_tensor_scan(
                    nr[:, vc * 512:(vc + 1) * 512], ones128b[:], remt[:],
                    carryR[:, vc:vc + 1], ALU.mult, ALU.add),
                  reads=["ones128b", "remt", f"carryR{vc}", "carryR"], writes=[f"nr{vc}"])
                V(lambda r, vc=vc: r.tensor_copy(carryR[:, vc:vc + 1],
                                                 nr[:, vc * 512 + 511:vc * 512 + 512]),
                  reads=[f"nr{vc}"], writes=[f"carryR{vc}"])
                V(lambda r, vc=vc: r.tensor_tensor_scan(
                    ni[:, vc * 512:(vc + 1) * 512], ones128b[:], immt[:],
                    carryI[:, vc:vc + 1], ALU.mult, ALU.add),
                  reads=["ones128b", "immt", f"carryI{vc}", "carryI"], writes=[f"ni{vc}"])
                V(lambda r, vc=vc: r.tensor_copy(carryI[:, vc:vc + 1],
                                                 ni[:, vc * 512 + 511:vc * 512 + 512]),
                  reads=[f"ni{vc}"], writes=[f"carryI{vc}"])
                V(lambda r, vc=vc: r.tensor_mul(sq1[:], nr[:, vc * 512:(vc + 1) * 512],
                                                nr[:, vc * 512:(vc + 1) * 512]),
                  reads=[f"nr{vc}"], writes=["sq1"])
                V(lambda r, vc=vc: r.tensor_mul(sq2[:], ni[:, vc * 512:(vc + 1) * 512],
                                                ni[:, vc * 512:(vc + 1) * 512]),
                  reads=[f"ni{vc}"], writes=["sq2"])
                V(lambda r: r.tensor_add(sq1[:], sq1[:], sq2[:]),
                  reads=["sq1", "sq2"], writes=["sq1"])
                TN(lambda r, vc=vc: r.matmul(psq[:], E2_sb[:, vc * 16:(vc + 1) * 16], sq1[:],
                                             start=(vc == 0), stop=(vc == 7)),
                   reads=["E2", "sq1"],
                   writes=["psq"] if vc in (0, 7) else ["psq_part"])
            V(lambda r: r.tensor_copy(sqs[:], psq[:]), reads=["psq"], writes=["sqs"])
            V(lambda r: r.tensor_mul(mst[:], sqs[:], invt[:]),
              reads=["sqs", "invt"], writes=["mst"])
            V(lambda r: r.tensor_mul(mst[:], mst[:], invt[:]),
              reads=["mst", "invt"], writes=["mst"])
            V(lambda r: r.tensor_scalar(mst[:], mst[:], 1.0 / 128.0, 1e-5, ALU.mult, ALU.add),
              reads=["mst"], writes=["mst"])
            SC(lambda r: r.activation(out=mst[:], in_=mst[:], func=AF.Sqrt),
               reads=["mst"], writes=["mst"])
            V(lambda r: r.reciprocal(mst[:], mst[:]), reads=["mst"], writes=["mst"])
            V(lambda r: r.tensor_mul(c_t[:], mst[:], invt[:]),
              reads=["mst", "invt"], writes=["c_t"])
            for vc in range(8):
                def ymm(r, vc=vc):
                    r.matmul(py[0:64, :], wre_b[0:64, :], nr[:, vc * 512:vc * 512 + 512][0:64, :],
                             start=True, stop=False)
                    r.matmul(py[0:64, :], wim_b[0:64, :], ni[:, vc * 512:vc * 512 + 512][0:64, :],
                             start=False, stop=True)
                    r.matmul(py[64:128, :], wre_b[64:128, :],
                             nr[:, vc * 512:vc * 512 + 512][64:128, :],
                             start=True, stop=False)
                    return r.matmul(py[64:128, :], wim_b[64:128, :],
                                    ni[:, vc * 512:vc * 512 + 512][64:128, :],
                                    start=False, stop=True)
                TN(ymm, reads=["wreb", "wimb", f"nr{vc}", f"ni{vc}"], writes=["py"])
                TN(lambda r, vc=vc: r.matmul(pcb[:], E1_sb[:, vc * 128:(vc + 1) * 128], c_t[:],
                                             start=True, stop=True),
                   reads=["E1", "c_t"], writes=["pcb"])
                V(lambda r, vc=vc: r.tensor_tensor(ytmp[:], py[:],
                                                   gateb[:, vc * 512:(vc + 1) * 512], ALU.mult),
                  reads=["py", f"gateb{vc}"], writes=["ytmp"])
                V(lambda r: r.tensor_copy(cbsb[:], pcb[:]), reads=["pcb"], writes=["cbsb"])
                V(lambda r, vc=vc: r.tensor_mul(ogb[:, vc * 512:(vc + 1) * 512], ytmp[:],
                                                cbsb[:]),
                  reads=["ytmp", "cbsb"], writes=[f"ogb{vc}"])
            for sub in range(4):
                def omm(r, sub=sub):
                    for half in range(2):
                        for vc in range(8):
                            mm = r.matmul(po[:, half * 512:(half + 1) * 512],
                                          ogb[:, vc * 512 + sub * 128:vc * 512 + sub * 128 + 128],
                                          wout_sb[:, vc * D + half * 512:vc * D + (half + 1) * 512],
                                          start=(vc == 0), stop=(vc == 7))
                    return mm
                TN(omm, reads=["wout"] + [f"ogb{v}" for v in range(8)], writes=["po"])
                ob = (sub % 2) * D
                ok = f"otile{sub % 2}"
                V(lambda r, ob=ob: r.tensor_copy(otile[:, ob:ob + 512], po[:, 0:512]),
                  reads=["po"], writes=[ok + "a"])
                V(lambda r, ob=ob: r.tensor_copy(otile[:, ob + 512:ob + D], po[:, 512:1024]),
                  reads=["po"], writes=[ok + "b"])
                r0 = i * 512 + sub * 128
                LD("scd", lambda r, r0=r0, ob=ob: r.dma_start(
                    out=s1p[r0:r0 + 128, :], in_=otile[:, ob:ob + D]),
                   reads=[ok + "a", ok + "b"], writes=[ok + "a", ok + "b"])
        P.finalize()

        block = ctx.enter_context(nc.Block())

        @block.sync
        def _(r):
            P.emit("syn", r, sems)

        @block.vector
        def _(r):
            P.emit("vec", r, sems)

        @block.scalar
        def _(r):
            P.emit(("sca", "scd"), r, sems)
            r.wait_ge(sems["scd"], P.total["scd"])

        @block.tensor
        def _(r):
            P.emit("ten", r, sems)

        @block.gpsimd
        def _(r):
            P.emit("gps", r, sems)
    return nc


def _f32(a):
    return np.ascontiguousarray(np.asarray(a, np.float32))


def _run(name, build_fn, in_maps):
    nc = _cached(name, build_fn)
    return run_bass_kernel_spmd(nc, in_maps, core_ids=list(range(8))).results


def kernel(**inputs):
    f32 = lambda k: np.ascontiguousarray(np.asarray(inputs[k], np.float32))
    x = f32("L0_attn_x").reshape(T, D)

    res = _run("ln", lambda: build_ln(TPC),
               [{"x": np.ascontiguousarray(x[c * TPC:(c + 1) * TPC])} for c in range(8)])
    xn = np.concatenate([res[c]["xn"] for c in range(8)], axis=0)

    xnT = np.ascontiguousarray(xn.T)
    wr = f32("L0_ffn_w_router")
    res = _run("router", lambda: build_router(TPC),
               [{"xnT": np.ascontiguousarray(xnT[:, c * TPC:(c + 1) * TPC]), "wr": wr}
                for c in range(8)])
    logits_tok = np.ascontiguousarray(
        np.concatenate([res[c]["lg"] for c in range(8)], axis=1).T)
    lg_tiled = np.ascontiguousarray(
        logits_tok.reshape(T // 128, 128, 8).transpose(1, 0, 2).reshape(128, (T // 128) * 8))

    res = _run("routing", build_routing, [{"lg": lg_tiled}] * 8)
    r0 = res[0]
    slot1, slot2 = r0["slot1"], r0["slot2"]
    w1h, w2h = r0["w1h"], r0["w2h"]
    srcmap = r0["srcmap"][:, 0]

    w1 = f32("L0_ffn_w1")
    w2 = f32("L0_ffn_w2")
    maps = []
    for e in range(8):
        maps.append({
            "xn": xn,
            "src": np.ascontiguousarray(
                srcmap[e * ECAP:(e + 1) * ECAP].reshape(ECAP // 128, 128).T),
            "w1": np.ascontiguousarray(w1[e]),
            "w2": np.ascontiguousarray(w2[e]),
        })
    res = _run("moe", build_moe_expert, maps)
    Y = np.zeros((65664, D), NB)
    for e in range(8):
        Y[e * ECAP:(e + 1) * ECAP] = res[e]["y"]

    maps = []
    for c in range(8):
        b, h = c // 2, c % 2
        xnT_b = np.ascontiguousarray(xn.reshape(B, S, D)[b].T)
        m = {"xnT": xnT_b,
             "xnTq": np.ascontiguousarray(xnT_b[:, h * 4096:(h + 1) * 4096]),
             "bvrows": f32("L0_attn_bv")}
        for nm in ("q", "k", "v"):
            m[f"U{nm}"] = f32(f"L0_attn_U{nm}")
            m[f"V{nm}"] = f32(f"L0_attn_V{nm}")
            m[f"b{nm}T"] = np.ascontiguousarray(f32(f"L0_attn_b{nm}").T)
        maps.append(m)
    res = _run("attn", build_attn, maps)
    s_fm = [res[c]["s"] for c in range(8)]

    bias = np.ascontiguousarray(f32("L0_ffn_bias").reshape(1, D))
    maps = []
    for c in range(8):
        cs = slice(c * (TPC // 128), (c + 1) * (TPC // 128))
        maps.append({
            "Y": Y,
            "slot1": np.ascontiguousarray(slot1[:, cs]),
            "slot2": np.ascontiguousarray(slot2[:, cs]),
            "w1h": np.ascontiguousarray(w1h[:, cs]),
            "w2h": np.ascontiguousarray(w2h[:, cs]),
            "bias": bias,
            "s": s_fm[c],
            "x": np.ascontiguousarray(x[c * TPC:(c + 1) * TPC]),
        })
    res = _run("combine", build_combine, maps)
    out1 = np.concatenate([res[c]["out1"] for c in range(8)], axis=0)
    xn1 = np.concatenate([res[c]["xn1"] for c in range(8)], axis=0)

    wup = f32("L1_ffn_w_up")
    wdn = f32("L1_ffn_w_down")
    maps = [{"wf": np.ascontiguousarray(wup if c % 2 == 0 else wdn.reshape(4096, D))}
            for c in range(8)]
    res = _run("l1prep", build_l1prep, maps)
    wupT = np.ascontiguousarray(res[0]["wt"].T)
    wdnT = np.ascontiguousarray(res[1]["wt"].reshape(D, 4096).T)

    wip = f32("L1_attn_in_proj_w")
    ck = f32("L1_attn_conv_k")[:, 0, :]
    cb = f32("L1_attn_conv_b")
    theta = f32("L1_attn_theta").reshape(2048)
    dsl = f32("L1_attn_decay_slopes")
    asl = f32("L1_attn_anchor_slopes")
    ssc = f32("L1_attn_score_scale")
    dlg = f32("L1_attn_deriv_logits")
    wre = f32("L1_attn_W_re")
    wim = f32("L1_attn_W_im")
    nsc = f32("L1_attn_norm_scale")
    wop = f32("L1_attn_out_proj_w")
    mi = np.arange(1024)
    E1 = (np.arange(16)[:, None] == 2 * (mi // 128) + (mi % 128) // 64).astype(np.float32)
    pp = np.arange(128)
    mm2 = np.arange(128)
    E2 = ((mm2 % 16)[None, :] == 2 * (mm2 // 16)[None, :] + (pp // 64)[:, None]).astype(np.float32)
    maps = []
    for c in range(8):
        b, kh = c // 2, c % 2
        gc = np.concatenate([
            np.arange(1024) + kh * 1024,
            np.arange(1024) + 2048 + kh * 1024,
            np.arange(16) + 4096 + kh * 16])
        ckt = np.zeros((128, 68), np.float32)
        cbt = np.zeros((128, 17), np.float32)
        for cc in range(17):
            w = 128 if cc < 16 else 16
            ch = gc[cc * 128:cc * 128 + w]
            for j in range(4):
                ckt[:w, j * 17 + cc] = ck[j, ch]
            cbt[:w, cc] = cb[ch]
        kg = np.arange(16) + kh * 16
        slope = np.where(kg < 28, dsl[np.minimum(kg, 27)], asl[np.maximum(kg - 28, 0)])
        maps.append({
            "xn1T": np.ascontiguousarray(xn1.reshape(B, S, D)[b].T),
            "winb": np.ascontiguousarray(wip[:, gc]),
            "ckt": ckt, "cbt": cbt,
            "thc": np.ascontiguousarray(theta[kh * 1024:(kh + 1) * 1024].reshape(8, 128).T),
            "slope": np.ascontiguousarray(slope.reshape(16, 1).astype(np.float32)),
            "isanch": np.ascontiguousarray((kg >= 28).astype(np.float32).reshape(16, 1)),
            "sscale": np.ascontiguousarray(ssc[kg].reshape(16, 1)),
            "dlog": np.ascontiguousarray(dlg.reshape(1, 3)),
            "wre": wre, "wim": wim,
            "nsc0": np.ascontiguousarray(nsc[:64].reshape(64, 1)),
            "nsc1": np.ascontiguousarray(nsc[64:].reshape(64, 1)),
            "wout": np.ascontiguousarray(wop[kh * 1024:(kh + 1) * 1024, :]),
            "E1": E1, "E2": E2,
        })
    res = _run("l1mix", build_l1mix, maps)
    s1pA = np.concatenate([res[2 * b]["s1p"] for b in range(B)], axis=0)
    s1pB = np.concatenate([res[2 * b + 1]["s1p"] for b in range(B)], axis=0)

    xn1T = np.ascontiguousarray(xn1.T)
    maps = []
    for c in range(8):
        sl = slice(c * TPC, (c + 1) * TPC)
        maps.append({
            "xn1T": np.ascontiguousarray(xn1T[:, sl]),
            "wupT": wupT, "wdnT": wdnT,
            "s1pa": np.ascontiguousarray(s1pA[sl]),
            "s1pb": np.ascontiguousarray(s1pB[sl]),
            "out1": np.ascontiguousarray(out1[sl]),
        })
    res = _run("l1ffn", build_l1ffn, maps)
    out = np.concatenate([res[c]["out"] for c in range(8)], axis=0)
    return out.reshape(B, S, D)
